# version 17
# baseline (speedup 1.0000x reference)
import numpy as np
import ml_dtypes
import concourse.bass as bass
import concourse.mybir as mybir
from concourse.bass_utils import run_bass_kernel_spmd

F32 = mybir.dt.float32
BF16 = mybir.dt.bfloat16
AF = mybir.ActivationFunctionType
ALU = mybir.AluOpType
AX = mybir.AxisListType

S = 4096
D = 1024
H = 16
HP = 8
NKB = 8
TKS = [8, 16, 24, 32]
QBG = {0: [0, 3, 4, 7], 1: [1, 2, 5, 6]}
EXP_W = [1024] * 4 + [2048] * 4
EXP_OFF = [0, 1024, 2048, 3072, 4096, 6144, 8192, 10240]
RD = 1.0 / D
KB = 1024


class Walk:

    def __init__(self, target, handles, sems):
        self.t = target
        self.h = handles
        self.sems = sems
        self.cnt = {k: 0 for k in sems}
        self.ev = {}
        self.free = {}
        self.lastwait = {}

    def op(self, eng, fn, inc=None, by=1):
        if eng == self.t:
            ins = fn(self.h[eng])
            if inc is not None:
                ins.then_inc(self.sems[inc], by)
        if inc is not None:
            self.cnt[inc] += by

    def mark(self, key, sem):
        self.ev[key] = (sem, self.cnt[sem])

    def wait_ev(self, eng, key):
        if key not in self.ev:
            return
        sem, val = self.ev[key]
        self.wait(eng, sem, val)

    def wait(self, eng, sem, val):
        if val <= 0 or eng != self.t:
            return
        if self.lastwait.get(sem, 0) >= val:
            return
        self.h[eng].wait_ge(self.sems[sem], val)
        self.lastwait[sem] = val

    def selfwait(self, eng, sem):
        self.wait(eng, sem, self.cnt[sem])

    def vdrain(self):
        if self.t == "v":
            self.h["v"].drain()

    def sdrain(self):
        if self.t == "s":
            self.h["s"].drain()

    def use(self, eng, res):
        key = self.free.get(res)
        if key is not None:
            self.wait_ev(eng, key)

    def release(self, res, key):
        self.free[res] = key


def build_nc(debug=False):
    nc = bass.Bass("TRN2", target_bir_lowering=False, debug=False)

    x_keys = nc.dram_tensor("x_keys", [S, D], F32, kind="ExternalInput").ap()
    x_own = nc.dram_tensor("x_own", [2048, D], F32, kind="ExternalInput").ap()
    wqkv_in = nc.dram_tensor("wqkv_in", [128, 8 * 3072], F32, kind="ExternalInput").ap()
    wout_in = nc.dram_tensor("wout_in", [128, 8 * KB], F32, kind="ExternalInput").ap()
    router_in = nc.dram_tensor("router_in", [128, 64], F32, kind="ExternalInput").ap()
    w1_in = nc.dram_tensor("w1_in", [D, 12288], F32, kind="ExternalInput").ap()
    w2_in = nc.dram_tensor("w2_in", [12288, D], F32, kind="ExternalInput").ap()
    masks_in = nc.dram_tensor("masks_in", [128, 4, 8, 512], BF16, kind="ExternalInput").ap()
    ident_bf_in = nc.dram_tensor("ident_bf_in", [128, 128], BF16, kind="ExternalInput").ap()
    ident_f32_in = nc.dram_tensor("ident_f32_in", [128, 128], F32, kind="ExternalInput").ap()
    out_d = nc.dram_tensor("out", [2048, D], F32, kind="ExternalOutput").ap()
    scr_kind = "ExternalOutput" if debug else "Internal"
    k_scr = nc.dram_tensor("k_scr", [8, NKB, 128, 512], BF16, kind=scr_kind).ap()
    v_scr = nc.dram_tensor("v_scr", [8, NKB, 4, 128, 196], BF16, kind=scr_kind).ap()
    if debug:
        dbg_q = nc.dram_tensor("dbg_q", [128, 8, 2048], BF16, kind="ExternalOutput").ap()
        dbg_ctx = nc.dram_tensor("dbg_ctx", [64, 16, 512], BF16, kind="ExternalOutput").ap()
        dbg_x2 = nc.dram_tensor("dbg_x2", [128, 16 * KB], F32, kind="ExternalOutput").ap()
        dbg_xn2 = nc.dram_tensor("dbg_xn2", [128, 8, 2048], BF16, kind="ExternalOutput").ap()
        dbg_probs = nc.dram_tensor("dbg_probs", [128, 128], F32, kind="ExternalOutput").ap()
        dbg_c = nc.dram_tensor("dbg_c", [128, 128], F32, kind="ExternalOutput").ap()
        dbg_stats = nc.dram_tensor("dbg_stats", [6, 128, 48], F32, kind="ExternalOutput").ap()
        dbg_tmp = nc.dram_tensor("dbg_tmp", [2, 128, KB], F32, kind="ExternalOutput").ap()
        dbg_stage = nc.dram_tensor("dbg_stage", [2, 128, 8, 128], F32, kind="ExternalOutput").ap()
        dbg_s = nc.dram_tensor("dbg_s", [128, 1024], F32, kind="ExternalOutput").ap()
        dbg_p = nc.dram_tensor("dbg_p", [128, 2, 512], BF16, kind="ExternalOutput").ap()

    SB_BASE = 16512

    def sb(name, off, shape, dtype):
        return nc.alloc_sbuf_tensor_at(name, shape, dtype, offset=SB_BASE + off).ap()

    ident_bf = sb("ident_bf", 0, [128, 128], BF16)
    ident_f32 = sb("ident_f32", 256, [128, 128], F32)
    eps_t = sb("eps_t", 768, [128, 8], F32)
    ones64f = sb("ones64f", 800, [128, 64], F32)
    router_sb = sb("router_sb", 1056, [128, 8, 8], F32)
    bn6 = sb("bn6", 1312, [128, 8, 12], F32)
    mv_sb = sb("mv_sb", 1728, [128, 8, 2], F32)
    rstd4 = sb("rstd4", 1824, [128, 8], F32)
    nmr4 = sb("nmr4", 1856, [128, 8], F32)
    probs_sb = sb("probs_sb", 2464, [128, 128], F32)
    c_sb = sb("c_sb", 2976, [128, 128], F32)
    msk1_t = sb("msk1_t", 3488, [128, 8], F32)
    p2_t = sb("p2_t", 3520, [128, 8], F32)
    m1c = sb("m1c", 3552, [128, 1], F32)
    m2c = sb("m2c", 3584, [128, 1], F32)
    swc = sb("swc", 3616, [128, 1], F32)
    rcc = sb("rcc", 3648, [128, 1], F32)

    wqkv_sb = sb("wqkv_sb", 4 * KB, [128, 8, 3072], BF16)
    x2_sb = sb("x2_sb", 4 * KB, [128, 16 * KB], F32)

    q_fm = sb("q_fm", 68 * KB, [128, 8, 2048], BF16)
    xn2_fm = sb("xn2_fm", 68 * KB, [128, 8, 2048], BF16)

    R = 100 * KB
    x_in = [sb(f"x_in{i}", R + i * 16 * KB, [128, 4, KB], F32) for i in range(2)]
    x_in.append(sb("x_in2", 52 * KB, [128, 4, KB], F32))
    xn_tm = [sb(f"xn_tm{i}", R + 32 * KB + i * 2 * KB, [128, KB], BF16) for i in range(2)]
    xn_fm = [sb(f"xn_fm{i}", R + 36 * KB + i * 8 * KB, [128, 8, 512], BF16) for i in range(3)]
    kstg = [sb(f"kstg{i}", R + 60 * KB + i * 8 * KB, [128, 8, 512], BF16) for i in range(2)]
    vstg = [sb(f"vstg{i}", R + 76 * KB + i * 12544, [128, 4, 8, 196], BF16) for i in range(2)]
    _c = [R]

    def _nxt(nbytes, align=64):
        off = (_c[0] + align - 1) // align * align
        _c[0] = off + nbytes
        return off

    kst = [sb(f"kst{i}", _nxt(1024), [128, 512], BF16) for i in range(6)]
    vst = [sb(f"vst{i}", _nxt(1568), [128, 4, 196], BF16) for i in range(6)]
    p_sb = [sb(f"p_sb{i}", _nxt(2048), [128, 2, 512], BF16) for i in range(3)]
    mask_sb = sb("mask_sb", _nxt(8192), [128, 8, 512], BF16)
    recip_sb = sb("recip_sb", _nxt(4096), [128, 2, 512], F32)
    recipb_sb = [sb(f"recipb_sb{i}", _nxt(2048), [64, 512], F32) for i in range(2)]
    xo_sb = [sb(f"xo_sb{i}", _nxt(4096), [128, KB], F32) for i in range(3)]
    ctx_sb = sb("ctx_sb", _nxt(8192), [128, 8, 512], BF16)
    ctxo_sb = sb("ctxo_sb", _nxt(8192), [64, 8, 512], BF16)
    wout_sb = sb("wout_sb", _nxt(16384), [128, 8, KB], BF16)
    assert _c[0] - R <= 99 * KB, _c[0] - R
    xn2tmp = [sb(f"xn2tmp{i}", R + 83 * KB + i * 4 * KB, [128, KB], F32) for i in range(3)]
    stage = [sb(f"stage{i}", R + 95 * KB + i * 4 * KB, [128, 8, 128], F32) for i in range(3)]
    w1_sb = sb("w1_sb", R, [128, 8, 2048], BF16)
    w2_sb = sb("w2_sb", R + 32 * KB, [128, 16, KB], BF16)
    h1_s = [sb(f"h1_s{i}", R + 64 * KB + i * KB, [128, 512], BF16) for i in range(2)]

    h2_sb = sb("h2_sb", R + 66 * KB, [128, 16, 512], BF16)

    cap_s = sb("cap_s", R + 94 * KB, [128, 1024], F32) if debug else None
    cap_p = sb("cap_p", R + 98 * KB, [128, 2, 512], BF16) if debug else None
    pall = nc.alloc_psum_tensor("pall", [128, 8, 512], F32).ap()

    def bank(i):
        return pall[:, i, :]

    import contextlib
    es = contextlib.ExitStack()
    sems = {}
    for name in ["T", "V", "Sc", "QD", "GD", "SO"]:
        sems[name] = es.enter_context(nc.semaphore(name))
    blk = es.enter_context(nc.Block())

    def walk(w: Walk):
        w.op("g", lambda g: g.dma_start(out=wqkv_sb[:, :, :],
                                        in_=wqkv_in.rearrange("p (c k) -> p c k", c=8)),
             inc="GD", by=16)
        w.mark("wqkv_ld", "GD")
        w.op("g", lambda g: g.dma_start(out=router_sb[:, :, :],
                                        in_=router_in.rearrange("p (c e) -> p c e", c=8)),
             inc="GD", by=16)
        w.mark("router_ld", "GD")
        w.op("q", lambda q: q.dma_start(out=ident_bf[:], in_=ident_bf_in), inc="QD", by=16)
        w.op("q", lambda q: q.dma_start(out=ident_f32[:], in_=ident_f32_in), inc="QD", by=16)
        w.mark("ident_ld", "QD")
        w.op("v", lambda v: v.memset(eps_t[:], 1e-5))
        w.op("v", lambda v: v.memset(ones64f[:], 1.0))
        w.op("v", lambda v: v.memset(vstg[0][:, :, :, 64:65], 1.0))
        w.op("v", lambda v: v.memset(vstg[1][:, :, :, 64:65], 1.0))
        w.op("v", lambda v: v.memset(vstg[0][:, :, :, 129:130], 1.0))
        w.op("v", lambda v: v.memset(vstg[1][:, :, :, 129:130], 1.0))
        w.op("v", lambda v: v.memset(vstg[0][:, :, :, 130:196], 0.0))
        w.op("v", lambda v: v.memset(vstg[1][:, :, :, 130:196], 0.0), inc="V")
        w.mark("consts_v", "V")
        if w.t == "s":
            from concourse.hw_specs import get_activation_tables
            idx = list(get_activation_tables(nc.m.arch)).index("natural_log_exp_and_others")
            w.h["s"].add_instruction(mybir.InstLoadActFuncSet(
                name=nc.get_next_instruction_name(), act_func_set_id=idx, ins=[], outs=[]))

        UNITS = []
        for _qb in range(4):
            for _hp in range(HP):
                for _kb2 in range(TKS[_qb] // 4):
                    UNITS.append((_hp, _kb2))

        def kv_load_g(u):
            sl2 = u % 6
            hp_u, kb2_u = UNITS[u]
            w.use("q", f"kst{sl2}")
            w.use("q", f"vst{sl2}")
            w.op("q", lambda q, hp_u=hp_u, kb2_u=kb2_u, sl2=sl2: q.dma_start(
                out=kst[sl2][:, :], in_=k_scr[hp_u, kb2_u]), inc="QD", by=16)
            w.op("q", lambda q, hp_u=hp_u, kb2_u=kb2_u, sl2=sl2: q.dma_start(
                out=vst[sl2][:, :, :],
                in_=v_scr[hp_u, kb2_u].rearrange("a p b -> p a b")), inc="QD", by=16)
            w.mark(f"kv_ld:{u}", "QD")

        def ln_stats_block(jbase, tiles, xin_ev):
            nt = len(tiles)
            c0 = ((jbase // 4) % 2) * 4
            w.wait_ev("v", xin_ev)
            w.wait_ev("v", f"lnapply_done:{jbase - 8}")
            for t, src_ap in enumerate(tiles):
                w.op("v", lambda v, t=c0 + t, src_ap=src_ap: v.bn_stats(
                    bn6[:, t, 0:6], src_ap[:, 0:512]))
                w.op("v", lambda v, t=c0 + t, src_ap=src_ap: v.bn_stats(
                    bn6[:, t, 6:12], src_ap[:, 512:1024]))
            w.vdrain()
            for t in range(nt):
                w.op("v", lambda v, t=c0 + t: v.bn_aggr(mv_sb[:, t, :], bn6[:, t, :]),
                     inc="V" if t == nt - 1 else None)
            w.mark(f"mv:{jbase}", "V")
            w.wait_ev("s", f"mv:{jbase}")
            mvv = mv_sb.rearrange("p a b -> p (a b)")
            w.op("s", lambda s, nt=nt, mvv=mvv, c0=c0: s.activation(
                rstd4[:, c0:c0 + nt], mvv[:, 2 * c0 + 1:2 * (c0 + nt):2],
                AF.Ln, bias=eps_t[:, 0:1]))
            w.sdrain()
            w.op("s", lambda s, nt=nt, c0=c0: s.activation(
                rstd4[:, c0:c0 + nt], rstd4[:, c0:c0 + nt], AF.Exp, scale=-0.5),
                inc="Sc")
            w.mark(f"rstdb:{jbase}", "Sc")
            w.wait_ev("v", f"rstdb:{jbase}")
            w.vdrain()
            w.op("v", lambda v, nt=nt, mvv=mvv, c0=c0: v.scalar_tensor_tensor(
                nmr4[:, c0:c0 + nt], mvv[:, 2 * c0:2 * (c0 + nt):2], -1.0,
                rstd4[:, c0:c0 + nt], ALU.mult, ALU.mult), inc="V")
            w.mark(f"nmrb:{jbase}", "V")

        def ln_apply(j, t, src_ap, xnslot, jbase):
            c0 = ((jbase // 4) % 2) * 4
            w.wait_ev("s", f"nmrb:{jbase}")
            w.use("s", f"xn_tm{xnslot}")
            w.op("s", lambda s, t=c0 + t, src_ap=src_ap, xnslot=xnslot: s.activation(
                xn_tm[xnslot][:], src_ap, AF.Identity,
                bias=nmr4[:, t:t + 1], scale=rstd4[:, t:t + 1]), inc="Sc")
            w.mark(f"apply:{j}", "Sc")

        def tr_evac(j, xnslot, bsl, fmslot, tb):
            trv = pall[:, bsl, :].bitcast(BF16).rearrange("p (c q) -> p c q", c=8)
            w.wait_ev("t", f"apply:{j}")
            w.use("t", f"bank{bsl}")
            w.wait("t", "QD", w.ev["ident_ld"][1])
            for c in range(8):
                w.op("t", lambda t, c=c: t.matmul(trv[:, c, :],
                                                  xn_tm[xnslot][:, c * 128:(c + 1) * 128],
                                                  ident_bf[:], start=(c == 0), stop=(c == 7),
                                                  is_transpose=True),
                     inc="T" if c == 7 else None)
            w.mark(f"tr:{j}", "T")
            w.release(f"xn_tm{xnslot}", f"tr:{j}")
            w.wait_ev("v", f"tr:{j}")
            w.use("v", f"xn_fm{fmslot}")
            w.op("v", lambda v: v.tensor_copy(xn_fm[fmslot][:, :, tb * 128:(tb + 1) * 128],
                                              trv[:, :, :]), inc="V")
            w.mark(f"xnfm:{j}", "V")
            w.release(f"bank{bsl}", f"xnfm:{j}")

        def p1_front(blk_i, is_keys):
            base_j = blk_i * 4
            src_dram = x_keys if is_keys else x_own
            row0 = blk_i * 512 if is_keys else (blk_i - NKB) * 512
            fmslot = blk_i % 3
            xsl = blk_i % 2 if blk_i < NKB else {8: 2, 9: 0, 10: 1, 11: 2}[blk_i]
            w.use("g", f"x_in{xsl}")
            w.op("g", lambda g, row0=row0, xsl=xsl: g.dma_start(
                out=x_in[xsl][:, :, :],
                in_=src_dram[row0: row0 + 512, :]
                .rearrange("(a p) d -> p a d", p=128)), inc="GD", by=16)
            w.mark(f"xin:{blk_i}", "GD")
            ln_stats_block(base_j, [x_in[xsl][:, t, :] for t in range(4)],
                           f"xin:{blk_i}")
            for t2 in range(4):
                j = base_j + t2
                ln_apply(j, t2, x_in[xsl][:, t2, :], j % 2, base_j)
                if t2 == 3:
                    w.release(f"x_in{xsl}", f"apply:{j}")
                    w.mark(f"lnapply_done:{base_j}", "Sc")
                tr_evac(j, j % 2, j % 2, fmslot, t2)

        def p1_back(blk_i, is_keys):
            base_j = blk_i * 4
            fmslot = blk_i % 3
            ksl = blk_i % 2
            w.wait_ev("t", f"xnfm:{base_j + 3}")
            w.wait("t", "GD", w.ev["wqkv_ld"][1])
            if is_keys:
                for hp in range(HP):
                    kb = 2 + hp % 2
                    w.use("t", f"bank{kb}")
                    for c in range(8):
                        w.op("t", lambda t, c=c, hp=hp, kb=kb: t.matmul(
                            bank(kb), wqkv_sb[:, c, 1024 + hp * 128: 1024 + (hp + 1) * 128],
                            xn_fm[fmslot][:, c, :], start=(c == 0), stop=(c == 7)),
                            inc="T" if c == 7 else None)
                    w.mark(f"kproj:{blk_i}:{hp}", "T")
                    w.wait_ev("s", f"kproj:{blk_i}:{hp}")
                    w.use("s", f"kstg{ksl}")
                    w.op("s", lambda s, hp=hp, kb=kb, ksl=ksl: s.copy(
                        kstg[ksl][:, hp, :], bank(kb)), inc="Sc")
                    w.mark(f"kevac:{blk_i}:{hp}", "Sc")
                    w.release(f"bank{kb}", f"kevac:{blk_i}:{hp}")
                for kt in range(4):
                    b0 = 4 + 2 * (kt % 2)
                    w.use("t", f"bank{b0}")
                    w.use("t", f"bank{b0 + 1}")
                    for half2 in range(2):
                        for c in range(8):
                            w.op("t", lambda t, c=c, kt=kt, half2=half2, b0=b0: t.matmul(
                                bank(b0 + half2), xn_fm[fmslot][:, c, kt * 128:(kt + 1) * 128],
                                wqkv_sb[:, c, 2048 + half2 * 512: 2048 + (half2 + 1) * 512],
                                start=(c == 0), stop=(c == 7)),
                                inc="T" if (c == 7 and half2 == 1) else None)
                    w.mark(f"vproj:{blk_i}:{kt}", "T")
                    w.wait_ev("v", f"vproj:{blk_i}:{kt}")
                    w.use("v", f"vstg{ksl}")
                    vsrc = pall[:, b0:b0 + 2, :].rearrange("p a b -> p (a b)").rearrange(
                        "p (a b dd) -> p a b dd", a=8, b=2)
                    w.op("v", lambda v, kt=kt, vsrc=vsrc, ksl=ksl: v.tensor_copy(
                        vstg[ksl][:, kt, :, 0:64], vsrc[:, :, 0, :]))
                    w.op("v", lambda v, kt=kt, vsrc=vsrc, ksl=ksl: v.tensor_copy(
                        vstg[ksl][:, kt, :, 65:129], vsrc[:, :, 1, :]), inc="V")
                    w.mark(f"vevac:{blk_i}:{kt}", "V")
                    w.release(f"bank{b0}", f"vevac:{blk_i}:{kt}")
                    w.release(f"bank{b0 + 1}", f"vevac:{blk_i}:{kt}")
                for hp in range(HP):
                    w.wait_ev("q", f"kevac:{blk_i}:{hp}")
                    w.op("q", lambda q, hp=hp, blk_i=blk_i, ksl=ksl: q.dma_start(
                        out=k_scr[hp, blk_i], in_=kstg[ksl][:, hp, :]),
                        inc="QD", by=16)
                    w.wait_ev("q", f"vevac:{blk_i}:3")
                    w.op("q", lambda q, hp=hp, blk_i=blk_i, ksl=ksl: q.dma_start(
                        out=v_scr[hp, blk_i].rearrange("a p b -> p a b"),
                        in_=vstg[ksl][:, :, hp, :]), inc="QD", by=16)
                w.mark(f"spill:{blk_i}", "QD")
                w.release(f"kstg{ksl}", f"spill:{blk_i}")
                w.release(f"vstg{ksl}", f"spill:{blk_i}")
            else:
                ob = blk_i - NKB
                for hp in range(HP):
                    kb = 2 + hp % 2
                    w.use("t", f"bank{kb}")
                    for c in range(8):
                        w.op("t", lambda t, c=c, hp=hp, kb=kb: t.matmul(
                            bank(kb), wqkv_sb[:, c, hp * 128:(hp + 1) * 128],
                            xn_fm[fmslot][:, c, :], start=(c == 0), stop=(c == 7)),
                            inc="T" if c == 7 else None)
                    w.mark(f"qproj:{blk_i}:{hp}", "T")
                    w.wait_ev("s", f"qproj:{blk_i}:{hp}")
                    w.op("s", lambda s, hp=hp, kb=kb, ob=ob: s.copy(
                        q_fm[:, hp, ob * 512:(ob + 1) * 512], bank(kb)), inc="Sc")
                    w.mark(f"qevac:{blk_i}:{hp}", "Sc")
                    w.release(f"bank{kb}", f"qevac:{blk_i}:{hp}")
            w.mark(f"proj_done:{blk_i}", "T")
            w.release(f"xn_fm{fmslot}", f"proj_done:{blk_i}")

        for blk_i in range(3):
            p1_front(blk_i, blk_i < NKB)
        for blk_i in range(NKB + 4):
            p1_back(blk_i, blk_i < NKB)
            if blk_i + 3 < NKB + 4:
                p1_front(blk_i + 3, blk_i + 3 < NKB)
        w.wait_ev("q", "apply:39")
        for u in range(6):
            kv_load_g(u)
        for sem_name, key in [("T", "ph1_T"), ("V", "ph1_V"), ("Sc", "ph1_S"),
                              ("QD", "ph1_QD"), ("GD", "ph1_GD")]:
            w.mark(key, sem_name)

        for eng in ["t", "v", "s", "q", "g"]:
            for key in ["ph1_T", "ph1_V", "ph1_S", "ph1_QD", "ph1_GD"]:
                w.wait_ev(eng, key)

        if debug:
            w.op("q", lambda q: q.dma_start(out=dbg_q, in_=q_fm[:, :, :]), inc="QD", by=16)

        w.op("g", lambda g: g.dma_start(out=wout_sb[:, :, :],
                                        in_=wout_in.rearrange("p (hh e) -> p hh e", hh=8)),
             inc="GD", by=16)
        w.mark("wout_ld", "GD")

        ks_seq = 0
        pending_epi = []
        pending_opj = []
        for qb in range(4):
            TK = TKS[qb]
            w.use("q", "mask_sb")
            w.op("q", lambda q, qb=qb: q.dma_start(out=mask_sb[:, :, :],
                                                   in_=masks_in[:, qb, :, :]), inc="QD", by=16)
            w.mark(f"mask_ld:{qb}", "QD")

            for hp in range(HP):
                seq0 = ks_seq
                ks_seq += TK // 4
                nkb2 = TK // 4


                sstate = {}

                def s_part(ktg, hp=hp, qb=qb, TK=TK, seq0=seq0):
                    kb2, kt = divmod(ktg, 4)
                    sl2 = (seq0 + kb2) % 6
                    ss = sstate[ktg] = (seq0 * 4 + ktg) % 3
                    sp = (seq0 * 4 + ktg) % 2
                    sb0 = 2 * sp
                    w.wait_ev("t", f"kv_ld:{seq0 + kb2}")
                    w.use("t", f"bank{sb0}")
                    w.use("t", f"bank{sb0 + 1}")
                    for side in range(2):
                        w.op("t", lambda t, side=side, b=sb0 + side, sl2=sl2,
                             kt=kt, hp=hp, qb=qb: t.matmul(
                                 bank(b),
                                 kst[sl2][side * 64:(side + 1) * 64, kt * 128:(kt + 1) * 128],
                                 q_fm[side * 64:(side + 1) * 64, hp, qb * 512:(qb + 1) * 512],
                                 start=True, stop=True),
                             inc="T" if side == 1 else None)
                    w.mark(f"s:{qb}:{hp}:{ktg}", "T")
                    w.wait_ev("s", f"s:{qb}:{hp}:{ktg}")
                    w.use("s", f"p{ss}")
                    esrc = pall[:, sb0:sb0 + 2, :].rearrange("p a b -> p (a b)")
                    edst = p_sb[ss].rearrange("p a b -> p (a b)")
                    w.op("s", lambda s, esrc=esrc, edst=edst: s.activation(
                        edst, esrc, AF.Exp, scale=0.125), inc="Sc")
                    w.mark(f"exp:{qb}:{hp}:{ktg}", "Sc")
                    w.release(f"bank{sb0}", f"exp:{qb}:{hp}:{ktg}")
                    w.release(f"bank{sb0 + 1}", f"exp:{qb}:{hp}:{ktg}")
                    slot = ktg - (TK - 8)
                    if slot >= 0:
                        w.wait_ev("v", f"exp:{qb}:{hp}:{ktg}")
                        w.wait("v", "QD", w.ev[f"mask_ld:{qb}"][1])
                        for hd in range(2):
                            w.op("v", lambda v, ss=ss, hd=hd, slot=slot: v.tensor_mul(
                                p_sb[ss][:, hd, :], p_sb[ss][:, hd, :],
                                mask_sb[:, slot, :]), inc="V" if hd == 1 else None)
                        w.mark(f"pmask:{qb}:{hp}:{ktg}", "V")

                def pv_part(ktg, hp=hp, qb=qb, TK=TK, seq0=seq0, nkb2=nkb2):
                    kb2, kt = divmod(ktg, 4)
                    sl2 = (seq0 + kb2) % 6
                    ss = sstate.pop(ktg)
                    if ktg - (TK - 8) >= 0:
                        w.wait_ev("t", f"pmask:{qb}:{hp}:{ktg}")
                    w.wait_ev("t", f"exp:{qb}:{hp}:{ktg}")
                    for hd in range(2):
                        if ktg == 0:
                            w.use("t", f"bank{6 + hd}")
                        w.op("t", lambda t, hd=hd, kt=kt, sl2=sl2, ss=ss,
                             ktg=ktg, TK=TK: t.matmul(
                                 pall[:, 6 + hd, :],
                                 vst[sl2][:, kt, hd * 65:hd * 65 + 128],
                                 p_sb[ss][:, hd, :],
                                 start=(ktg == 0), stop=(ktg == TK - 1)),
                             inc="T" if hd == 1 else None)
                    w.mark(f"pv:{qb}:{hp}:{ktg}", "T")
                    w.release(f"p{ss}", f"pv:{qb}:{hp}:{ktg}")
                    if kt == 3 or ktg == TK - 1:
                        w.release(f"kst{sl2}", f"pv:{qb}:{hp}:{ktg}")
                        w.release(f"vst{sl2}", f"pv:{qb}:{hp}:{ktg}")
                        if seq0 + kb2 + 6 < len(UNITS):
                            kv_load_g(seq0 + kb2 + 6)

                for k2 in range(0, TK, 2):
                    s_part(k2)
                    s_part(k2 + 1)
                    if k2 == 0 and pending_epi:
                        pending_epi.pop(0)()
                    if k2 >= 2 and pending_opj:
                        pending_opj.pop(0)()
                    if k2 >= 2:
                        pv_part(k2 - 2)
                        pv_part(k2 - 1)
                while pending_opj:
                    pending_opj.pop(0)()
                pv_part(TK - 2)
                pv_part(TK - 1)
                for hd in range(2):
                    cb = 6 + hd
                    w.wait_ev("s", f"pv:{qb}:{hp}:{TK - 1}")
                    w.use("s", f"recip{hd}")
                    w.op("s", lambda s, hd=hd, cb=cb: s.activation(
                        recip_sb[64:65, hd, :], pall[64:65, cb, :], AF.Ln))
                    w.sdrain()
                    w.op("s", lambda s, hd=hd: s.activation(
                        recip_sb[64:65, hd, :], recip_sb[64:65, hd, :], AF.Exp,
                        scale=-1.0), inc="Sc")
                    w.mark(f"recip:{qb}:{hp}:{hd}", "Sc")

                def epi_tv(qb=qb, hp=hp):
                    if hp == 0:
                        w.use("v", "ctx_sb")
                    for hd in range(2):
                        cb = 6 + hd
                        h = 2 * hp + hd
                        w.wait_ev("t", f"recip:{qb}:{hp}:{hd}")
                        w.use("t", f"bank{4 + hd}")
                        w.op("t", lambda t, hd=hd: t.matmul(
                            pall[0:64, 4 + hd, :], ones64f[64:65, 0:64],
                            recip_sb[64:65, hd, :], start=True, stop=True), inc="T")
                        w.mark(f"rb:{qb}:{hp}:{hd}", "T")
                        w.release(f"recip{hd}", f"rb:{qb}:{hp}:{hd}")
                        w.wait_ev("v", f"rb:{qb}:{hp}:{hd}")
                        w.use("v", f"recipb{hd}")
                        w.vdrain()
                        w.op("v", lambda v, hd=hd: v.tensor_copy(recipb_sb[hd][0:64, :],
                                                                 pall[0:64, 4 + hd, :]))
                        w.vdrain()
                        cdst = (ctx_sb[0:64, hp, :] if hd == 0 else ctxo_sb[0:64, hp, :])
                        w.op("v", lambda v, hd=hd, cb=cb, cdst=cdst: v.tensor_tensor(
                            cdst, pall[0:64, cb, :], recipb_sb[hd][0:64, :],
                            ALU.mult), inc="V")
                        w.mark(f"ctxevac:{qb}:{hp}:{hd}", "V")
                        w.release(f"recipb{hd}", f"ctxevac:{qb}:{hp}:{hd}")
                        w.release(f"bank{4 + hd}", f"ctxevac:{qb}:{hp}:{hd}")
                        w.release(f"bank{cb}", f"ctxevac:{qb}:{hp}:{hd}")
                    w.mark(f"ctx_done:{qb}:{hp}", "V")

                pending_epi.append(epi_tv)
            w.release("mask_sb", f"pmask:{qb}:7:{TK - 1}")
            def opj_chain(idx, qb=qb):
                qt, eh = divmod(idx, 2)
                if True:
                    ab = 4 + idx % 2
                    xsl = qt % 3
                    if eh == 0:
                        w.use("q", f"xo{xsl}")
                        w.op("q", lambda q, qb=qb, qt=qt, xsl=xsl: q.dma_start(
                            out=xo_sb[xsl][:, :],
                            in_=x_own[qb * 512 + qt * 128: qb * 512 + (qt + 1) * 128, :]),
                            inc="QD", by=16)
                        w.mark(f"xo_ld:{qb}:{qt}", "QD")
                    if idx == 0:
                        w.wait("g", "V", w.ev[f"ctx_done:{qb}:7"][1])
                        w.op("g", lambda g: g.dma_start(out=ctx_sb[64:128, :, :],
                                                        in_=ctxo_sb[0:64, :, :]),
                             inc="GD", by=16)
                        w.mark(f"repack:{qb}", "GD")
                    w.wait("t", "GD", w.ev[f"repack:{qb}"][1])
                    w.use("t", f"bank{ab}")
                    for hp2 in range(8):
                        w.op("t", lambda t, hp2=hp2, qt=qt, eh=eh, ab=ab: t.matmul(
                            bank(ab), ctx_sb[:, hp2, qt * 128:(qt + 1) * 128],
                            wout_sb[:, hp2, eh * 512:(eh + 1) * 512],
                            start=(hp2 == 0), stop=(hp2 == 7)), inc="T" if hp2 == 7 else None)
                    w.mark(f"opj:{qb}:{qt * 2 + eh}", "T")
                    w.wait_ev("v", f"opj:{qb}:{qt * 2 + eh}")
                    w.wait("v", "QD", w.ev[f"xo_ld:{qb}:{qt}"][1])
                    w.op("v", lambda v, qb=qb, qt=qt, eh=eh, ab=ab, xsl=xsl: v.tensor_add(
                        x2_sb[:, (qb * 4 + qt) * KB + eh * 512:
                              (qb * 4 + qt) * KB + (eh + 1) * 512],
                        bank(ab), xo_sb[xsl][:, eh * 512:(eh + 1) * 512]), inc="V")
                    w.mark(f"x2:{qb}:{qt * 2 + eh}", "V")
                    w.release(f"bank{ab}", f"x2:{qb}:{qt * 2 + eh}")
                    if eh == 1:
                        w.release(f"xo{xsl}", f"x2:{qb}:{qt * 2 + eh}")
                    if idx == 7:
                        w.release("ctx_sb", f"opj:{qb}:7")

            for idx in range(8):
                pending_opj.append(lambda idx=idx, qb=qb: opj_chain(idx, qb))
            if qb == 3:
                while pending_epi:
                    pending_epi.pop(0)()
                while pending_opj:
                    pending_opj.pop(0)()
        for sem_name, key in [("T", "ph2_T"), ("V", "ph2_V"), ("Sc", "ph2_S"),
                              ("QD", "ph2_QD"), ("GD", "ph2_GD")]:
            w.mark(key, sem_name)
        for eng in ["t", "v", "s", "q", "g"]:
            for key in ["ph2_T", "ph2_V", "ph2_S", "ph2_QD", "ph2_GD"]:
                w.wait_ev(eng, key)

        if debug:
            w.op("q", lambda q: q.dma_start(out=dbg_ctx, in_=ctx_sb[:, :, :]), inc="SO", by=16)
            w.mark("dbg_ctx", "SO")
            w.op("q", lambda q: q.dma_start(out=dbg_x2, in_=x2_sb[:]), inc="QD", by=16)
            w.mark("dbg_x2", "QD")

        wd0, off0 = EXP_W[0], EXP_OFF[0]
        w.use("g", "w1_sb")
        w.op("g", lambda g: g.dma_start(
            out=w1_sb[:, :, 0:wd0],
            in_=w1_in.rearrange("(c p) wdim -> p c wdim", p=128)[:, :, off0:off0 + wd0]),
            inc="GD", by=16)
        w.mark("w1_ld:0", "GD")
        w.use("g", "w2_sb")
        w.op("g", lambda g: g.dma_start(
            out=w2_sb[:, 0:wd0 // 128, :],
            in_=w2_in.rearrange("(wc p) d -> p wc d", p=128)[:, 0:wd0 // 128, :]),
            inc="GD", by=16)
        w.mark("w2_ld:0", "GD")

        for tt in range(16):
            sl = tt % 3
            src = x2_sb[:, tt * KB:(tt + 1) * KB]
            w.wait_ev("v", f"apply2:{tt - 3}")
            w.op("v", lambda v, sl=sl, src=src: v.bn_stats(bn6[:, sl, 0:6], src[:, 0:512]))
            w.op("v", lambda v, sl=sl, src=src: v.bn_stats(bn6[:, sl, 6:12], src[:, 512:1024]))
            w.vdrain()
            w.op("v", lambda v, sl=sl: v.bn_aggr(mv_sb[:, sl, :], bn6[:, sl, :]), inc="V")
            w.mark(f"mv2:{tt}", "V")
            w.wait_ev("s", f"mv2:{tt}")
            w.op("s", lambda s, sl=sl: s.activation(rstd4[:, sl:sl + 1], mv_sb[:, sl, 1:2],
                                                    AF.Ln, bias=eps_t[:, 0:1]))
            w.sdrain()
            w.op("s", lambda s, sl=sl: s.activation(rstd4[:, sl:sl + 1], rstd4[:, sl:sl + 1],
                                                    AF.Exp, scale=-0.5), inc="Sc")
            w.mark(f"rstd2:{tt}", "Sc")
            w.wait_ev("v", f"rstd2:{tt}")
            w.vdrain()
            w.op("v", lambda v, sl=sl: v.scalar_tensor_tensor(
                nmr4[:, sl:sl + 1], mv_sb[:, sl, 0:1], -1.0, rstd4[:, sl:sl + 1],
                ALU.mult, ALU.mult), inc="V")
            w.mark(f"nmr2:{tt}", "V")
            w.wait_ev("s", f"nmr2:{tt}")
            w.use("s", f"xn2tmp{sl}")
            w.op("s", lambda s, sl=sl, src=src: s.activation(xn2tmp[sl][:], src, AF.Identity,
                                                    bias=nmr4[:, sl:sl + 1],
                                                    scale=rstd4[:, sl:sl + 1]), inc="Sc")
            w.mark(f"apply2:{tt}", "Sc")
            w.wait_ev("t", f"apply2:{tt}")
            w.wait_ev("t", f"stage_f32:{tt - 3}")
            for c in range(8):
                b = 2 * sl + c // 4
                if c % 4 == 0:
                    w.use("t", f"bank{b}")
                off = (c % 4) * 128
                w.op("t", lambda t, c=c, b=b, off=off, sl=sl: t.matmul(
                    pall[:, b, off:off + 128], xn2tmp[sl][:, c * 128:(c + 1) * 128],
                    ident_f32[:], start=(c % 4 == 0), stop=(c % 4 == 3),
                    is_transpose=True), inc="T" if c == 7 else None)
            w.mark(f"tr2:{tt}", "T")
            w.release(f"xn2tmp{sl}", f"tr2:{tt}")
            w.wait_ev("v", f"tr2:{tt}")
            w.use("v", f"stage{sl}")
            psrc = pall[:, 2 * sl:2 * sl + 2, :].rearrange("p a b -> p (a b)").rearrange(
                "p (c q) -> p c q", c=8)
            w.op("v", lambda v, psrc=psrc, sl=sl: v.tensor_copy(stage[sl][:, :, :], psrc),
                 inc="V")
            w.mark(f"stage_f32:{tt}", "V")
            w.op("v", lambda v, psrc=psrc, tt=tt: v.tensor_copy(
                xn2_fm[:, :, tt * 128:(tt + 1) * 128], psrc), inc="V")
            w.mark(f"xn2cast:{tt}", "V")
            w.release(f"bank{2 * sl}", f"xn2cast:{tt}")
            w.release(f"bank{2 * sl + 1}", f"xn2cast:{tt}")
            w.wait_ev("t", f"stage_f32:{tt}")
            w.wait("t", "GD", w.ev["router_ld"][1])
            if tt == 0:
                w.use("t", "bank6")
            for c in range(8):
                w.op("t", lambda t, c=c, sl=sl, tt=tt: t.matmul(
                    pall[:, 6, tt * 8:(tt + 1) * 8], stage[sl][:, c, :], router_sb[:, c, :],
                    start=(c == 0 and tt == 0), stop=(c == 7 and tt == 15)),
                    inc="T" if c == 7 else None)
            w.mark(f"router:{tt}", "T")
            w.release(f"stage{sl}", f"router:{tt}")
        w.wait_ev("s", "router:15")
        if w.t == "s":
            from concourse.hw_specs import get_activation_tables
            sidx = list(get_activation_tables(nc.m.arch)).index("sigmoid_and_others")
            w.h["s"].add_instruction(mybir.InstLoadActFuncSet(
                name=nc.get_next_instruction_name(), act_func_set_id=sidx, ins=[], outs=[]))
        w.op("s", lambda s: s.activation(probs_sb[:], pall[:, 6, 0:128], AF.Sigmoid), inc="Sc")
        w.mark("sigmoid", "Sc")
        w.release("bank6", "sigmoid")
        for tt in range(16):
            pr = probs_sb[:, tt * 8:(tt + 1) * 8]
            w.wait_ev("v", "sigmoid")
            w.op("v", lambda v, pr=pr: v.reduce_max(m1c[:], pr, axis=AX.X), inc="V")
            w.selfwait("v", "V")
            w.op("v", lambda v, pr=pr: v.tensor_single_scalar(msk1_t[:], pr, m1c[:, 0:1],
                                                              ALU.is_equal))
            w.vdrain()
            w.op("v", lambda v, pr=pr: v.scalar_tensor_tensor(p2_t[:], msk1_t[:], -2.0, pr,
                                                              ALU.mult, ALU.add), inc="V")
            w.selfwait("v", "V")
            w.op("v", lambda v: v.reduce_max(m2c[:], p2_t[:], axis=AX.X), inc="V")
            w.selfwait("v", "V")
            w.op("v", lambda v: v.tensor_single_scalar(p2_t[:], p2_t[:], m2c[:, 0:1],
                                                       ALU.is_equal))
            w.vdrain()
            w.op("v", lambda v: v.tensor_add(msk1_t[:], msk1_t[:], p2_t[:]))
            w.vdrain()
            w.vdrain()
            w.op("v", lambda v: v.scalar_tensor_tensor(swc[:], m1c[:], 1e-20, m2c[:],
                                                       ALU.add, ALU.add), inc="V")
            w.selfwait("v", "V")
            w.op("v", lambda v: v.reciprocal(rcc[:], swc[:]), inc="V")
            w.selfwait("v", "V")
            w.vdrain()
            w.op("v", lambda v, pr=pr, tt=tt: v.scalar_tensor_tensor(
                c_sb[:, tt * 8:(tt + 1) * 8], msk1_t[:], rcc[:, 0:1], pr, ALU.mult, ALU.mult),
                inc="V")
            w.mark(f"c:{tt}", "V")
        for sem_name, key in [("T", "ph4_T"), ("V", "ph4_V"), ("Sc", "ph4_S")]:
            w.mark(key, sem_name)
        for eng in ["t", "v", "s", "q", "g"]:
            for key in ["ph4_T", "ph4_V", "ph4_S"]:
                w.wait_ev(eng, key)

        if debug:
            w.op("q", lambda q: q.dma_start(out=dbg_s, in_=cap_s[:]), inc="QD", by=16)
            w.op("q", lambda q: q.dma_start(out=dbg_p, in_=cap_p[:, :, :]), inc="QD", by=16)
            w.op("q", lambda q: q.dma_start(out=dbg_xn2, in_=xn2_fm[:, :, :]), inc="QD", by=16)
            w.op("q", lambda q: q.dma_start(out=dbg_probs, in_=probs_sb[:]), inc="QD", by=16)
            w.op("q", lambda q: q.dma_start(out=dbg_c, in_=c_sb[:]), inc="QD", by=16)
            for i in range(2):
                w.op("q", lambda q, i=i: q.dma_start(out=dbg_tmp[i], in_=xn2tmp[i][:, :]),
                     inc="QD", by=16)
                w.op("q", lambda q, i=i: q.dma_start(out=dbg_stage[i], in_=stage[i][:, :, :]),
                     inc="QD", by=16)
            w.mark("dbg_ph4", "QD")
            w.wait_ev("v", "dbg_x2")
            w.wait_ev("v", "dbg_ph4")
            w.wait_ev("g", "dbg_ph4")
            w.wait_ev("g", "dbg_ctx")

        for e in range(8):
            wd = EXP_W[e]
            off = EXP_OFF[e]
            nwc = wd // 128
            if e == 0:
                continue_loads = False
            else:
                continue_loads = True
            if continue_loads:
                w.use("g", "w1_sb")
                w.op("g", lambda g, off=off, wd=wd: g.dma_start(
                    out=w1_sb[:, :, 0:wd],
                    in_=w1_in.rearrange("(c p) wdim -> p c wdim", p=128)[:, :, off:off + wd]),
                    inc="GD", by=16)
                w.mark(f"w1_ld:{e}", "GD")
                w.use("g", "w2_sb")
                w.op("g", lambda g, off=off, nwc=nwc: g.dma_start(
                    out=w2_sb[:, 0:nwc, :],
                    in_=w2_in.rearrange("(wc p) d -> p wc d", p=128)[:, off // 128: off // 128 + nwc, :]),
                    inc="GD", by=16)
                w.mark(f"w2_ld:{e}", "GD")
            for tb in range(4):
                for wc in range(nwc):
                    hb = wc % 2
                    w.wait("t", "GD", w.ev[f"w1_ld:{e}"][1])
                    w.use("t", f"bank{hb}")
                    for c in range(8):
                        w.op("t", lambda t, c=c, wc=wc, tb=tb, hb=hb: t.matmul(
                            bank(hb), w1_sb[:, c, wc * 128:(wc + 1) * 128],
                            xn2_fm[:, c, tb * 512:(tb + 1) * 512],
                            start=(c == 0), stop=(c == 7)), inc="T" if c == 7 else None)
                    w.mark(f"hmm:{e}:{tb}:{wc}", "T")
                    if tb == 3 and wc == nwc - 1:
                        w.release("w1_sb", f"hmm:{e}:{tb}:{wc}")
                    w.wait_ev("s", f"hmm:{e}:{tb}:{wc}")
                    w.use("s", f"h1{hb}")
                    w.op("s", lambda s, hb=hb: s.activation(h1_s[hb][:], bank(hb), AF.Relu),
                         inc="Sc")
                    w.mark(f"relu:{e}:{tb}:{wc}", "Sc")
                    w.release(f"bank{hb}", f"relu:{e}:{tb}:{wc}")
                    w.wait_ev("v", f"relu:{e}:{tb}:{wc}")
                    if wc == 0:
                        w.use("v", "h2_sb")
                    w.op("v", lambda v, wc=wc, hb=hb: v.tensor_mul(
                        h2_sb[:, wc, :], h1_s[hb][:], h1_s[hb][:]), inc="V")
                    w.mark(f"h2sq:{e}:{tb}:{wc}", "V")
                    w.release(f"h1{hb}", f"h2sq:{e}:{tb}:{wc}")
                for tt in range(4):
                    ttg = tb * 4 + tt
                    for dh in range(2):
                        yb = 2 + (tt * 2 + dh) % 2
                        w.wait_ev("t", f"h2sq:{e}:{tb}:{nwc - 1}")
                        w.wait("t", "GD", w.ev[f"w2_ld:{e}"][1])
                        w.use("t", f"bank{yb}")
                        for wc in range(nwc):
                            w.op("t", lambda t, wc=wc, tt=tt, dh=dh, yb=yb, nwc=nwc: t.matmul(
                                bank(yb), h2_sb[:, wc, tt * 128:(tt + 1) * 128],
                                w2_sb[:, wc, dh * 512:(dh + 1) * 512],
                                start=(wc == 0), stop=(wc == nwc - 1)),
                                inc="T" if wc == nwc - 1 else None)
                        w.mark(f"ymm:{e}:{tb}:{tt * 2 + dh}", "T")
                        if tb == 3 and tt == 3 and dh == 1:
                            w.release("w2_sb", f"ymm:{e}:{tb}:{tt * 2 + dh}")
                        w.wait_ev("v", f"ymm:{e}:{tb}:{tt * 2 + dh}")
                        w.op("v", lambda v, e=e, ttg=ttg, dh=dh, yb=yb: v.scalar_tensor_tensor(
                            x2_sb[:, ttg * KB + dh * 512: ttg * KB + (dh + 1) * 512],
                            bank(yb), c_sb[:, ttg * 8 + e: ttg * 8 + e + 1],
                            x2_sb[:, ttg * KB + dh * 512: ttg * KB + (dh + 1) * 512],
                            ALU.mult, ALU.add), inc="V")
                        w.mark(f"ysc:{e}:{ttg}:{dh}", "V")
                        w.release(f"bank{yb}", f"ysc:{e}:{ttg}:{dh}")
                w.release("h2_sb", f"ymm:{e}:{tb}:7")

        for ttg in range(16):
            w.wait_ev("q", f"ysc:7:{ttg}:1")
            w.op("q", lambda q, ttg=ttg: q.dma_start(
                out=out_d[ttg * 128:(ttg + 1) * 128, :],
                in_=x2_sb[:, ttg * KB:(ttg + 1) * KB]), inc="SO", by=16)
        w.wait("q", "SO", 16 * 16)

    @blk.sync
    def _(q):
        walk(Walk("q", {"q": q}, sems))

    @blk.gpsimd
    def _(g):
        walk(Walk("g", {"g": g}, sems))

    @blk.vector
    def _(v):
        walk(Walk("v", {"v": v}, sems))

    @blk.scalar
    def _(s):
        walk(Walk("s", {"s": s}, sems))

    @blk.tensor
    def _(t):
        walk(Walk("t", {"t": t}, sems))

    es.close()
    return nc


def _stage_inputs(L0_attn_x, L0_attn_w_qkv, L0_attn_w_out, L0_ffn_router_w,
                  L0_ffn_w1, L0_ffn_w2):
    x = np.ascontiguousarray(np.asarray(L0_attn_x, dtype=np.float32))
    w_qkv = np.asarray(L0_attn_w_qkv, dtype=np.float32)
    w_out = np.asarray(L0_attn_w_out, dtype=np.float32)
    router_w = np.asarray(L0_ffn_router_w, dtype=np.float32)
    w1 = np.ascontiguousarray(np.asarray(L0_ffn_w1, dtype=np.float32))
    w2 = np.ascontiguousarray(np.asarray(L0_ffn_w2, dtype=np.float32))

    wq = w_qkv[:, :, 0:64].reshape(D, 1024)
    wk = w_qkv[:, :, 64:128].reshape(D, 1024)
    wv = w_qkv[:, :, 128:192].reshape(D, 1024)
    wqkv = np.concatenate([wq, wk, wv], axis=1)
    wqkv_st = np.ascontiguousarray(
        wqkv.reshape(8, 128, 3072).transpose(1, 0, 2).reshape(128, 8 * 3072))
    wout_st = np.ascontiguousarray(
        w_out.reshape(8, 2, 64, KB).transpose(1, 2, 0, 3).reshape(128, 8 * KB))
    router_st = np.ascontiguousarray(
        router_w.T.reshape(8, 128, 8).transpose(1, 0, 2).reshape(128, 64))
    ident_bf = np.eye(128, dtype=ml_dtypes.bfloat16)
    ident_f32 = np.eye(128, dtype=np.float32)

    masks = {}
    ki = np.arange(128)[:, None]
    qj = np.arange(512)[None, :]
    for r in (0, 1):
        m = np.zeros((128, 4, 8, 512), dtype=ml_dtypes.bfloat16)
        for i in range(4):
            q0 = QBG[r][i] * 512
            for sl in range(8):
                ktg = TKS[i] - 8 + sl
                k0 = ktg * 128
                m[:, i, sl, :] = ((k0 + ki) <= (q0 + qj)).astype(ml_dtypes.bfloat16)
        masks[r] = m

    in_maps, row_maps = [], []
    for c in range(8):
        b, r = c // 2, c % 2
        rows = np.concatenate([g * 512 + np.arange(512) for g in QBG[r]])
        row_maps.append((b, rows))
        in_maps.append({
            "x_keys": x[b],
            "x_own": np.ascontiguousarray(x[b][rows]),
            "wqkv_in": wqkv_st,
            "wout_in": wout_st,
            "router_in": router_st,
            "w1_in": w1,
            "w2_in": w2,
            "masks_in": masks[r],
            "ident_bf_in": ident_bf,
            "ident_f32_in": ident_f32,
        })
    return in_maps, row_maps, x.shape


def kernel(L0_attn_x, L0_attn_w_qkv, L0_attn_w_out, L0_ffn_router_w,
           L0_ffn_w1, L0_ffn_w2):
    in_maps, row_maps, xshape = _stage_inputs(
        L0_attn_x, L0_attn_w_qkv, L0_attn_w_out, L0_ffn_router_w, L0_ffn_w1, L0_ffn_w2)
    nc = build_nc()
    res = run_bass_kernel_spmd(nc, in_maps, core_ids=list(range(8)))
    out = np.empty(xshape, dtype=np.float32)
    for c in range(8):
        b, rows = row_maps[c]
        out[b][rows] = np.asarray(res.results[c]["out"], dtype=np.float32)
    return out


# version 18
# speedup vs baseline: 1.0120x; 1.0120x over previous
import numpy as np
import ml_dtypes
import concourse.bass as bass
import concourse.mybir as mybir
from concourse.bass_utils import run_bass_kernel_spmd

F32 = mybir.dt.float32
BF16 = mybir.dt.bfloat16
AF = mybir.ActivationFunctionType
ALU = mybir.AluOpType
AX = mybir.AxisListType

S = 4096
D = 1024
H = 16
HP = 8
NKB = 8
TKS = [8, 16, 24, 32]
QBG = {0: [0, 3, 4, 7], 1: [1, 2, 5, 6]}
EXP_W = [1024] * 4 + [2048] * 4
EXP_OFF = [0, 1024, 2048, 3072, 4096, 6144, 8192, 10240]
RD = 1.0 / D
KB = 1024


class Walk:

    def __init__(self, target, handles, sems):
        self.t = target
        self.h = handles
        self.sems = sems
        self.cnt = {k: 0 for k in sems}
        self.ev = {}
        self.free = {}
        self.lastwait = {}

    def op(self, eng, fn, inc=None, by=1):
        if eng == self.t:
            ins = fn(self.h[eng])
            if inc is not None:
                ins.then_inc(self.sems[inc], by)
        if inc is not None:
            self.cnt[inc] += by

    def mark(self, key, sem):
        self.ev[key] = (sem, self.cnt[sem])

    def wait_ev(self, eng, key):
        if key not in self.ev:
            return
        sem, val = self.ev[key]
        self.wait(eng, sem, val)

    def wait(self, eng, sem, val):
        if val <= 0 or eng != self.t:
            return
        if self.lastwait.get(sem, 0) >= val:
            return
        self.h[eng].wait_ge(self.sems[sem], val)
        self.lastwait[sem] = val

    def selfwait(self, eng, sem):
        self.wait(eng, sem, self.cnt[sem])

    def vdrain(self):
        if self.t == "v":
            self.h["v"].drain()

    def sdrain(self):
        if self.t == "s":
            self.h["s"].drain()

    def use(self, eng, res):
        key = self.free.get(res)
        if key is not None:
            self.wait_ev(eng, key)

    def release(self, res, key):
        self.free[res] = key


def build_nc(debug=False):
    nc = bass.Bass("TRN2", target_bir_lowering=False, debug=False)

    x_keys = nc.dram_tensor("x_keys", [S, D], F32, kind="ExternalInput").ap()
    x_own = nc.dram_tensor("x_own", [2048, D], F32, kind="ExternalInput").ap()
    wqkv_in = nc.dram_tensor("wqkv_in", [128, 8 * 3072], F32, kind="ExternalInput").ap()
    wout_in = nc.dram_tensor("wout_in", [128, 8 * KB], F32, kind="ExternalInput").ap()
    router_in = nc.dram_tensor("router_in", [128, 64], F32, kind="ExternalInput").ap()
    w1_in = nc.dram_tensor("w1_in", [D, 12288], F32, kind="ExternalInput").ap()
    w2_in = nc.dram_tensor("w2_in", [12288, D], F32, kind="ExternalInput").ap()
    masks_in = nc.dram_tensor("masks_in", [128, 4, 8, 512], BF16, kind="ExternalInput").ap()
    ident_bf_in = nc.dram_tensor("ident_bf_in", [128, 128], BF16, kind="ExternalInput").ap()
    ident_f32_in = nc.dram_tensor("ident_f32_in", [128, 128], F32, kind="ExternalInput").ap()
    out_d = nc.dram_tensor("out", [2048, D], F32, kind="ExternalOutput").ap()
    scr_kind = "ExternalOutput" if debug else "Internal"
    k_scr = nc.dram_tensor("k_scr", [8, NKB, 128, 512], BF16, kind=scr_kind).ap()
    v_scr = nc.dram_tensor("v_scr", [8, NKB, 4, 128, 196], BF16, kind=scr_kind).ap()
    if debug:
        dbg_q = nc.dram_tensor("dbg_q", [128, 8, 2048], BF16, kind="ExternalOutput").ap()
        dbg_ctx = nc.dram_tensor("dbg_ctx", [64, 16, 512], BF16, kind="ExternalOutput").ap()
        dbg_x2 = nc.dram_tensor("dbg_x2", [128, 16 * KB], F32, kind="ExternalOutput").ap()
        dbg_xn2 = nc.dram_tensor("dbg_xn2", [128, 8, 2048], BF16, kind="ExternalOutput").ap()
        dbg_probs = nc.dram_tensor("dbg_probs", [128, 128], F32, kind="ExternalOutput").ap()
        dbg_c = nc.dram_tensor("dbg_c", [128, 128], F32, kind="ExternalOutput").ap()
        dbg_stats = nc.dram_tensor("dbg_stats", [6, 128, 48], F32, kind="ExternalOutput").ap()
        dbg_tmp = nc.dram_tensor("dbg_tmp", [2, 128, KB], F32, kind="ExternalOutput").ap()
        dbg_stage = nc.dram_tensor("dbg_stage", [2, 128, 8, 128], F32, kind="ExternalOutput").ap()
        dbg_s = nc.dram_tensor("dbg_s", [128, 1024], F32, kind="ExternalOutput").ap()
        dbg_p = nc.dram_tensor("dbg_p", [128, 2, 512], BF16, kind="ExternalOutput").ap()

    SB_BASE = 16512

    def sb(name, off, shape, dtype):
        return nc.alloc_sbuf_tensor_at(name, shape, dtype, offset=SB_BASE + off).ap()

    ident_bf = sb("ident_bf", 0, [128, 128], BF16)
    ident_f32 = sb("ident_f32", 256, [128, 128], F32)
    eps_t = sb("eps_t", 768, [128, 8], F32)
    ones64f = sb("ones64f", 800, [128, 64], F32)
    router_sb = sb("router_sb", 1056, [128, 8, 8], F32)
    bn6 = sb("bn6", 1312, [128, 8, 12], F32)
    mv_sb = sb("mv_sb", 1728, [128, 8, 2], F32)
    rstd4 = sb("rstd4", 1824, [128, 8], F32)
    nmr4 = sb("nmr4", 1856, [128, 8], F32)
    probs_sb = sb("probs_sb", 2464, [128, 128], F32)
    c_sb = sb("c_sb", 2976, [128, 128], F32)
    msk1_t = sb("msk1_t", 3488, [128, 8], F32)
    p2_t = sb("p2_t", 3520, [128, 8], F32)
    m1c = sb("m1c", 3552, [128, 1], F32)
    m2c = sb("m2c", 3584, [128, 1], F32)
    swc = sb("swc", 3616, [128, 1], F32)
    rcc = sb("rcc", 3648, [128, 1], F32)

    wqkv_sb = sb("wqkv_sb", 4 * KB, [128, 8, 3072], BF16)
    x2_sb = sb("x2_sb", 4 * KB, [128, 16 * KB], F32)

    q_fm = sb("q_fm", 68 * KB, [128, 8, 2048], BF16)
    xn2_fm = sb("xn2_fm", 68 * KB, [128, 8, 2048], BF16)

    R = 100 * KB
    x_in = [sb(f"x_in{i}", R + i * 16 * KB, [128, 4, KB], F32) for i in range(2)]
    x_in.append(sb("x_in2", 52 * KB, [128, 4, KB], F32))
    xn_tm = [sb(f"xn_tm{i}", R + 32 * KB + i * 2 * KB, [128, KB], BF16) for i in range(2)]
    xn_fm = [sb(f"xn_fm{i}", R + 36 * KB + i * 8 * KB, [128, 8, 512], BF16) for i in range(3)]
    kstg = [sb(f"kstg{i}", R + 60 * KB + i * 8 * KB, [128, 8, 512], BF16) for i in range(2)]
    vstg = [sb(f"vstg{i}", R + 76 * KB + i * 12544, [128, 4, 8, 196], BF16) for i in range(2)]
    _c = [R]

    def _nxt(nbytes, align=64):
        off = (_c[0] + align - 1) // align * align
        _c[0] = off + nbytes
        return off

    kst = [sb(f"kst{i}", _nxt(1024), [128, 512], BF16) for i in range(6)]
    vst = [sb(f"vst{i}", _nxt(1568), [128, 4, 196], BF16) for i in range(6)]
    p_sb = [sb(f"p_sb{i}", _nxt(2048), [128, 2, 512], BF16) for i in range(3)]
    mask_sb = sb("mask_sb", _nxt(8192), [128, 8, 512], BF16)
    recip_sb = sb("recip_sb", _nxt(4096), [128, 2, 512], F32)
    recipb_sb = [sb(f"recipb_sb{i}", _nxt(2048), [64, 512], F32) for i in range(2)]
    xo_sb = [sb(f"xo_sb{i}", _nxt(4096), [128, KB], F32) for i in range(3)]
    ctx_sb = sb("ctx_sb", _nxt(8192), [128, 8, 512], BF16)
    ctxo_sb = sb("ctxo_sb", _nxt(8192), [64, 8, 512], BF16)
    wout_sb = sb("wout_sb", _nxt(16384), [128, 8, KB], BF16)
    assert _c[0] - R <= 99 * KB, _c[0] - R
    xn2tmp = [sb(f"xn2tmp{i}", R + 83 * KB + i * 4 * KB, [128, KB], F32) for i in range(3)]
    stage = [sb(f"stage{i}", R + 95 * KB + i * 4 * KB, [128, 8, 128], F32) for i in range(3)]
    w1_sb = sb("w1_sb", R, [128, 8, 2048], BF16)
    w2_sb = sb("w2_sb", R + 32 * KB, [128, 16, KB], BF16)
    h1_s = [sb(f"h1_s{i}", R + 64 * KB + i * KB, [128, 512], BF16) for i in range(2)]

    h2_sb = sb("h2_sb", R + 66 * KB, [128, 16, 512], BF16)

    cap_s = sb("cap_s", R + 94 * KB, [128, 1024], F32) if debug else None
    cap_p = sb("cap_p", R + 98 * KB, [128, 2, 512], BF16) if debug else None
    pall = nc.alloc_psum_tensor("pall", [128, 8, 512], F32).ap()

    def bank(i):
        return pall[:, i, :]

    import contextlib
    es = contextlib.ExitStack()
    sems = {}
    for name in ["T", "V", "Sc", "QD", "GD", "SO"]:
        sems[name] = es.enter_context(nc.semaphore(name))
    blk = es.enter_context(nc.Block())

    def walk(w: Walk):
        w.op("g", lambda g: g.dma_start(out=wqkv_sb[:, :, :],
                                        in_=wqkv_in.rearrange("p (c k) -> p c k", c=8)),
             inc="GD", by=16)
        w.mark("wqkv_ld", "GD")
        w.op("g", lambda g: g.dma_start(out=router_sb[:, :, :],
                                        in_=router_in.rearrange("p (c e) -> p c e", c=8)),
             inc="GD", by=16)
        w.mark("router_ld", "GD")
        w.op("q", lambda q: q.dma_start(out=ident_bf[:], in_=ident_bf_in), inc="QD", by=16)
        w.op("q", lambda q: q.dma_start(out=ident_f32[:], in_=ident_f32_in), inc="QD", by=16)
        w.mark("ident_ld", "QD")
        w.op("v", lambda v: v.memset(eps_t[:], 1e-5))
        w.op("v", lambda v: v.memset(ones64f[:], 1.0))
        w.op("v", lambda v: v.memset(vstg[0][:, :, :, 64:65], 1.0))
        w.op("v", lambda v: v.memset(vstg[1][:, :, :, 64:65], 1.0))
        w.op("v", lambda v: v.memset(vstg[0][:, :, :, 129:130], 1.0))
        w.op("v", lambda v: v.memset(vstg[1][:, :, :, 129:130], 1.0))
        w.op("v", lambda v: v.memset(vstg[0][:, :, :, 130:196], 0.0))
        w.op("v", lambda v: v.memset(vstg[1][:, :, :, 130:196], 0.0), inc="V")
        w.mark("consts_v", "V")
        if w.t == "s":
            from concourse.hw_specs import get_activation_tables
            idx = list(get_activation_tables(nc.m.arch)).index("natural_log_exp_and_others")
            w.h["s"].add_instruction(mybir.InstLoadActFuncSet(
                name=nc.get_next_instruction_name(), act_func_set_id=idx, ins=[], outs=[]))

        UNITS = []
        for _qb in range(4):
            for _hp in range(HP):
                for _kb2 in range(TKS[_qb] // 4):
                    UNITS.append((_hp, _kb2))

        def kv_load_g(u):
            sl2 = u % 6
            hp_u, kb2_u = UNITS[u]
            w.use("q", f"kst{sl2}")
            w.use("q", f"vst{sl2}")
            w.op("q", lambda q, hp_u=hp_u, kb2_u=kb2_u, sl2=sl2: q.dma_start(
                out=kst[sl2][:, :], in_=k_scr[hp_u, kb2_u]), inc="QD", by=16)
            w.op("q", lambda q, hp_u=hp_u, kb2_u=kb2_u, sl2=sl2: q.dma_start(
                out=vst[sl2][:, :, :],
                in_=v_scr[hp_u, kb2_u].rearrange("a p b -> p a b")), inc="QD", by=16)
            w.mark(f"kv_ld:{u}", "QD")

        def ln_stats_block(jbase, tiles, xin_ev):
            nt = len(tiles)
            c0 = ((jbase // 4) % 2) * 4
            w.wait_ev("v", xin_ev)
            w.wait_ev("v", f"lnapply_done:{jbase - 8}")
            for t, src_ap in enumerate(tiles):
                w.op("v", lambda v, t=c0 + t, src_ap=src_ap: v.bn_stats(
                    bn6[:, t, 0:6], src_ap[:, 0:512]))
                w.op("v", lambda v, t=c0 + t, src_ap=src_ap: v.bn_stats(
                    bn6[:, t, 6:12], src_ap[:, 512:1024]))
            w.vdrain()
            for t in range(nt):
                w.op("v", lambda v, t=c0 + t: v.bn_aggr(mv_sb[:, t, :], bn6[:, t, :]),
                     inc="V" if t == nt - 1 else None)
            w.mark(f"mv:{jbase}", "V")
            w.wait_ev("s", f"mv:{jbase}")
            mvv = mv_sb.rearrange("p a b -> p (a b)")
            w.op("s", lambda s, nt=nt, mvv=mvv, c0=c0: s.activation(
                rstd4[:, c0:c0 + nt], mvv[:, 2 * c0 + 1:2 * (c0 + nt):2],
                AF.Ln, bias=eps_t[:, 0:1]))
            w.sdrain()
            w.op("s", lambda s, nt=nt, c0=c0: s.activation(
                rstd4[:, c0:c0 + nt], rstd4[:, c0:c0 + nt], AF.Exp, scale=-0.5),
                inc="Sc")
            w.mark(f"rstdb:{jbase}", "Sc")
            w.wait_ev("v", f"rstdb:{jbase}")
            w.vdrain()
            w.op("v", lambda v, nt=nt, mvv=mvv, c0=c0: v.scalar_tensor_tensor(
                nmr4[:, c0:c0 + nt], mvv[:, 2 * c0:2 * (c0 + nt):2], -1.0,
                rstd4[:, c0:c0 + nt], ALU.mult, ALU.mult), inc="V")
            w.mark(f"nmrb:{jbase}", "V")

        def ln_apply(j, t, src_ap, xnslot, jbase):
            c0 = ((jbase // 4) % 2) * 4
            w.wait_ev("s", f"nmrb:{jbase}")
            w.use("s", f"xn_tm{xnslot}")
            w.op("s", lambda s, t=c0 + t, src_ap=src_ap, xnslot=xnslot: s.activation(
                xn_tm[xnslot][:], src_ap, AF.Identity,
                bias=nmr4[:, t:t + 1], scale=rstd4[:, t:t + 1]), inc="Sc")
            w.mark(f"apply:{j}", "Sc")

        def tr_evac(j, xnslot, bsl, fmslot, tb):
            trv = pall[:, bsl, :].bitcast(BF16).rearrange("p (c q) -> p c q", c=8)
            w.wait_ev("t", f"apply:{j}")
            w.use("t", f"bank{bsl}")
            w.wait("t", "QD", w.ev["ident_ld"][1])
            for c in range(8):
                w.op("t", lambda t, c=c: t.matmul(trv[:, c, :],
                                                  xn_tm[xnslot][:, c * 128:(c + 1) * 128],
                                                  ident_bf[:], start=(c == 0), stop=(c == 7),
                                                  is_transpose=True),
                     inc="T" if c == 7 else None)
            w.mark(f"tr:{j}", "T")
            w.release(f"xn_tm{xnslot}", f"tr:{j}")
            w.wait_ev("v", f"tr:{j}")
            w.use("v", f"xn_fm{fmslot}")
            w.op("v", lambda v: v.tensor_copy(xn_fm[fmslot][:, :, tb * 128:(tb + 1) * 128],
                                              trv[:, :, :]), inc="V")
            w.mark(f"xnfm:{j}", "V")
            w.release(f"bank{bsl}", f"xnfm:{j}")

        def p1_front(blk_i, is_keys):
            base_j = blk_i * 4
            src_dram = x_keys if is_keys else x_own
            row0 = blk_i * 512 if is_keys else (blk_i - NKB) * 512
            fmslot = blk_i % 3
            xsl = blk_i % 2 if blk_i < NKB else {8: 2, 9: 0, 10: 1, 11: 2}[blk_i]
            w.use("g", f"x_in{xsl}")
            w.op("g", lambda g, row0=row0, xsl=xsl: g.dma_start(
                out=x_in[xsl][:, :, :],
                in_=src_dram[row0: row0 + 512, :]
                .rearrange("(a p) d -> p a d", p=128)), inc="GD", by=16)
            w.mark(f"xin:{blk_i}", "GD")
            ln_stats_block(base_j, [x_in[xsl][:, t, :] for t in range(4)],
                           f"xin:{blk_i}")
            for t2 in range(4):
                j = base_j + t2
                ln_apply(j, t2, x_in[xsl][:, t2, :], j % 2, base_j)
                if t2 == 3:
                    w.release(f"x_in{xsl}", f"apply:{j}")
                    w.mark(f"lnapply_done:{base_j}", "Sc")
                tr_evac(j, j % 2, j % 2, fmslot, t2)

        def p1_back(blk_i, is_keys):
            base_j = blk_i * 4
            fmslot = blk_i % 3
            ksl = blk_i % 2
            w.wait_ev("t", f"xnfm:{base_j + 3}")
            w.wait("t", "GD", w.ev["wqkv_ld"][1])
            if is_keys:
                for hp in range(HP):
                    kb = 2 + hp % 2
                    w.use("t", f"bank{kb}")
                    for c in range(8):
                        w.op("t", lambda t, c=c, hp=hp, kb=kb: t.matmul(
                            bank(kb), wqkv_sb[:, c, 1024 + hp * 128: 1024 + (hp + 1) * 128],
                            xn_fm[fmslot][:, c, :], start=(c == 0), stop=(c == 7)),
                            inc="T" if c == 7 else None)
                    w.mark(f"kproj:{blk_i}:{hp}", "T")
                    w.wait_ev("s", f"kproj:{blk_i}:{hp}")
                    w.use("s", f"kstg{ksl}")
                    w.op("s", lambda s, hp=hp, kb=kb, ksl=ksl: s.copy(
                        kstg[ksl][:, hp, :], bank(kb)), inc="Sc")
                    w.mark(f"kevac:{blk_i}:{hp}", "Sc")
                    w.release(f"bank{kb}", f"kevac:{blk_i}:{hp}")
                for kt in range(4):
                    b0 = 4 + 2 * (kt % 2)
                    w.use("t", f"bank{b0}")
                    w.use("t", f"bank{b0 + 1}")
                    for half2 in range(2):
                        for c in range(8):
                            w.op("t", lambda t, c=c, kt=kt, half2=half2, b0=b0: t.matmul(
                                bank(b0 + half2), xn_fm[fmslot][:, c, kt * 128:(kt + 1) * 128],
                                wqkv_sb[:, c, 2048 + half2 * 512: 2048 + (half2 + 1) * 512],
                                start=(c == 0), stop=(c == 7)),
                                inc="T" if (c == 7 and half2 == 1) else None)
                    w.mark(f"vproj:{blk_i}:{kt}", "T")
                    w.wait_ev("s", f"vproj:{blk_i}:{kt}")
                    w.use("s", f"vstg{ksl}")
                    w.wait("s", "V", w.ev["consts_v"][1])
                    vsrc = pall[:, b0:b0 + 2, :].rearrange("p a b -> p (a b)").rearrange(
                        "p (a b dd) -> p a b dd", a=8, b=2)
                    w.op("s", lambda s, kt=kt, vsrc=vsrc, ksl=ksl: s.copy(
                        vstg[ksl][:, kt, :, 0:64], vsrc[:, :, 0, :]))
                    w.op("s", lambda s, kt=kt, vsrc=vsrc, ksl=ksl: s.copy(
                        vstg[ksl][:, kt, :, 65:129], vsrc[:, :, 1, :]), inc="Sc")
                    w.mark(f"vevac:{blk_i}:{kt}", "Sc")
                    w.release(f"bank{b0}", f"vevac:{blk_i}:{kt}")
                    w.release(f"bank{b0 + 1}", f"vevac:{blk_i}:{kt}")
                for hp in range(HP):
                    w.wait_ev("q", f"kevac:{blk_i}:{hp}")
                    w.op("q", lambda q, hp=hp, blk_i=blk_i, ksl=ksl: q.dma_start(
                        out=k_scr[hp, blk_i], in_=kstg[ksl][:, hp, :]),
                        inc="QD", by=16)
                    w.wait_ev("q", f"vevac:{blk_i}:3")
                    w.op("q", lambda q, hp=hp, blk_i=blk_i, ksl=ksl: q.dma_start(
                        out=v_scr[hp, blk_i].rearrange("a p b -> p a b"),
                        in_=vstg[ksl][:, :, hp, :]), inc="QD", by=16)
                w.mark(f"spill:{blk_i}", "QD")
                w.release(f"kstg{ksl}", f"spill:{blk_i}")
                w.release(f"vstg{ksl}", f"spill:{blk_i}")
            else:
                ob = blk_i - NKB
                for hp in range(HP):
                    kb = 2 + hp % 2
                    w.use("t", f"bank{kb}")
                    for c in range(8):
                        w.op("t", lambda t, c=c, hp=hp, kb=kb: t.matmul(
                            bank(kb), wqkv_sb[:, c, hp * 128:(hp + 1) * 128],
                            xn_fm[fmslot][:, c, :], start=(c == 0), stop=(c == 7)),
                            inc="T" if c == 7 else None)
                    w.mark(f"qproj:{blk_i}:{hp}", "T")
                    w.wait_ev("s", f"qproj:{blk_i}:{hp}")
                    w.op("s", lambda s, hp=hp, kb=kb, ob=ob: s.copy(
                        q_fm[:, hp, ob * 512:(ob + 1) * 512], bank(kb)), inc="Sc")
                    w.mark(f"qevac:{blk_i}:{hp}", "Sc")
                    w.release(f"bank{kb}", f"qevac:{blk_i}:{hp}")
            w.mark(f"proj_done:{blk_i}", "T")
            w.release(f"xn_fm{fmslot}", f"proj_done:{blk_i}")

        for blk_i in range(3):
            p1_front(blk_i, blk_i < NKB)
        for blk_i in range(NKB + 4):
            p1_back(blk_i, blk_i < NKB)
            if blk_i + 3 < NKB + 4:
                p1_front(blk_i + 3, blk_i + 3 < NKB)
        w.wait_ev("q", "apply:39")
        for u in range(6):
            kv_load_g(u)
        for sem_name, key in [("T", "ph1_T"), ("V", "ph1_V"), ("Sc", "ph1_S"),
                              ("QD", "ph1_QD"), ("GD", "ph1_GD")]:
            w.mark(key, sem_name)

        for eng in ["t", "v", "s", "q", "g"]:
            for key in ["ph1_T", "ph1_V", "ph1_S", "ph1_QD", "ph1_GD"]:
                w.wait_ev(eng, key)

        if debug:
            w.op("q", lambda q: q.dma_start(out=dbg_q, in_=q_fm[:, :, :]), inc="QD", by=16)

        w.op("g", lambda g: g.dma_start(out=wout_sb[:, :, :],
                                        in_=wout_in.rearrange("p (hh e) -> p hh e", hh=8)),
             inc="GD", by=16)
        w.mark("wout_ld", "GD")

        ks_seq = 0
        pending_epi = []
        pending_opj = []
        for qb in range(4):
            TK = TKS[qb]
            w.use("q", "mask_sb")
            w.op("q", lambda q, qb=qb: q.dma_start(out=mask_sb[:, :, :],
                                                   in_=masks_in[:, qb, :, :]), inc="QD", by=16)
            w.mark(f"mask_ld:{qb}", "QD")

            for hp in range(HP):
                seq0 = ks_seq
                ks_seq += TK // 4
                nkb2 = TK // 4


                sstate = {}

                def s_part(ktg, hp=hp, qb=qb, TK=TK, seq0=seq0):
                    kb2, kt = divmod(ktg, 4)
                    sl2 = (seq0 + kb2) % 6
                    ss = sstate[ktg] = (seq0 * 4 + ktg) % 3
                    sp = (seq0 * 4 + ktg) % 2
                    sb0 = 2 * sp
                    w.wait_ev("t", f"kv_ld:{seq0 + kb2}")
                    w.use("t", f"bank{sb0}")
                    w.use("t", f"bank{sb0 + 1}")
                    for side in range(2):
                        w.op("t", lambda t, side=side, b=sb0 + side, sl2=sl2,
                             kt=kt, hp=hp, qb=qb: t.matmul(
                                 bank(b),
                                 kst[sl2][side * 64:(side + 1) * 64, kt * 128:(kt + 1) * 128],
                                 q_fm[side * 64:(side + 1) * 64, hp, qb * 512:(qb + 1) * 512],
                                 start=True, stop=True),
                             inc="T" if side == 1 else None)
                    w.mark(f"s:{qb}:{hp}:{ktg}", "T")
                    w.wait_ev("s", f"s:{qb}:{hp}:{ktg}")
                    w.use("s", f"p{ss}")
                    esrc = pall[:, sb0:sb0 + 2, :].rearrange("p a b -> p (a b)")
                    edst = p_sb[ss].rearrange("p a b -> p (a b)")
                    w.op("s", lambda s, esrc=esrc, edst=edst: s.activation(
                        edst, esrc, AF.Exp, scale=0.125), inc="Sc")
                    w.mark(f"exp:{qb}:{hp}:{ktg}", "Sc")
                    w.release(f"bank{sb0}", f"exp:{qb}:{hp}:{ktg}")
                    w.release(f"bank{sb0 + 1}", f"exp:{qb}:{hp}:{ktg}")
                    slot = ktg - (TK - 8)
                    if slot >= 0:
                        w.wait_ev("v", f"exp:{qb}:{hp}:{ktg}")
                        w.wait("v", "QD", w.ev[f"mask_ld:{qb}"][1])
                        for hd in range(2):
                            w.op("v", lambda v, ss=ss, hd=hd, slot=slot: v.tensor_mul(
                                p_sb[ss][:, hd, :], p_sb[ss][:, hd, :],
                                mask_sb[:, slot, :]), inc="V" if hd == 1 else None)
                        w.mark(f"pmask:{qb}:{hp}:{ktg}", "V")

                def pv_part(ktg, hp=hp, qb=qb, TK=TK, seq0=seq0, nkb2=nkb2):
                    kb2, kt = divmod(ktg, 4)
                    sl2 = (seq0 + kb2) % 6
                    ss = sstate.pop(ktg)
                    if ktg - (TK - 8) >= 0:
                        w.wait_ev("t", f"pmask:{qb}:{hp}:{ktg}")
                    w.wait_ev("t", f"exp:{qb}:{hp}:{ktg}")
                    for hd in range(2):
                        if ktg == 0:
                            w.use("t", f"bank{6 + hd}")
                        w.op("t", lambda t, hd=hd, kt=kt, sl2=sl2, ss=ss,
                             ktg=ktg, TK=TK: t.matmul(
                                 pall[:, 6 + hd, :],
                                 vst[sl2][:, kt, hd * 65:hd * 65 + 128],
                                 p_sb[ss][:, hd, :],
                                 start=(ktg == 0), stop=(ktg == TK - 1)),
                             inc="T" if hd == 1 else None)
                    w.mark(f"pv:{qb}:{hp}:{ktg}", "T")
                    w.release(f"p{ss}", f"pv:{qb}:{hp}:{ktg}")
                    if kt == 3 or ktg == TK - 1:
                        w.release(f"kst{sl2}", f"pv:{qb}:{hp}:{ktg}")
                        w.release(f"vst{sl2}", f"pv:{qb}:{hp}:{ktg}")
                        if seq0 + kb2 + 6 < len(UNITS):
                            kv_load_g(seq0 + kb2 + 6)

                for k2 in range(0, TK, 2):
                    s_part(k2)
                    s_part(k2 + 1)
                    if k2 == 0 and pending_epi:
                        pending_epi.pop(0)()
                    if k2 >= 2 and pending_opj:
                        pending_opj.pop(0)()
                    if k2 >= 2:
                        pv_part(k2 - 2)
                        pv_part(k2 - 1)
                while pending_opj:
                    pending_opj.pop(0)()
                pv_part(TK - 2)
                pv_part(TK - 1)
                for hd in range(2):
                    cb = 6 + hd
                    w.wait_ev("s", f"pv:{qb}:{hp}:{TK - 1}")
                    w.use("s", f"recip{hd}")
                    w.op("s", lambda s, hd=hd, cb=cb: s.activation(
                        recip_sb[64:65, hd, :], pall[64:65, cb, :], AF.Ln))
                    w.sdrain()
                    w.op("s", lambda s, hd=hd: s.activation(
                        recip_sb[64:65, hd, :], recip_sb[64:65, hd, :], AF.Exp,
                        scale=-1.0), inc="Sc")
                    w.mark(f"recip:{qb}:{hp}:{hd}", "Sc")

                def epi_tv(qb=qb, hp=hp):
                    if hp == 0:
                        w.use("v", "ctx_sb")
                    for hd in range(2):
                        cb = 6 + hd
                        h = 2 * hp + hd
                        w.wait_ev("t", f"recip:{qb}:{hp}:{hd}")
                        w.use("t", f"bank{4 + hd}")
                        w.op("t", lambda t, hd=hd: t.matmul(
                            pall[0:64, 4 + hd, :], ones64f[64:65, 0:64],
                            recip_sb[64:65, hd, :], start=True, stop=True), inc="T")
                        w.mark(f"rb:{qb}:{hp}:{hd}", "T")
                        w.release(f"recip{hd}", f"rb:{qb}:{hp}:{hd}")
                        w.wait_ev("v", f"rb:{qb}:{hp}:{hd}")
                        w.use("v", f"recipb{hd}")
                        w.vdrain()
                        w.op("v", lambda v, hd=hd: v.tensor_copy(recipb_sb[hd][0:64, :],
                                                                 pall[0:64, 4 + hd, :]))
                        w.vdrain()
                        cdst = (ctx_sb[0:64, hp, :] if hd == 0 else ctxo_sb[0:64, hp, :])
                        w.op("v", lambda v, hd=hd, cb=cb, cdst=cdst: v.tensor_tensor(
                            cdst, pall[0:64, cb, :], recipb_sb[hd][0:64, :],
                            ALU.mult), inc="V")
                        w.mark(f"ctxevac:{qb}:{hp}:{hd}", "V")
                        w.release(f"recipb{hd}", f"ctxevac:{qb}:{hp}:{hd}")
                        w.release(f"bank{4 + hd}", f"ctxevac:{qb}:{hp}:{hd}")
                        w.release(f"bank{cb}", f"ctxevac:{qb}:{hp}:{hd}")
                    w.mark(f"ctx_done:{qb}:{hp}", "V")

                pending_epi.append(epi_tv)
            w.release("mask_sb", f"pmask:{qb}:7:{TK - 1}")
            def opj_chain(idx, qb=qb):
                qt, eh = divmod(idx, 2)
                if True:
                    ab = 4 + idx % 2
                    xsl = qt % 3
                    if eh == 0:
                        w.use("q", f"xo{xsl}")
                        w.op("q", lambda q, qb=qb, qt=qt, xsl=xsl: q.dma_start(
                            out=xo_sb[xsl][:, :],
                            in_=x_own[qb * 512 + qt * 128: qb * 512 + (qt + 1) * 128, :]),
                            inc="QD", by=16)
                        w.mark(f"xo_ld:{qb}:{qt}", "QD")
                    if idx == 0:
                        w.wait("g", "V", w.ev[f"ctx_done:{qb}:7"][1])
                        w.op("g", lambda g: g.dma_start(out=ctx_sb[64:128, :, :],
                                                        in_=ctxo_sb[0:64, :, :]),
                             inc="GD", by=16)
                        w.mark(f"repack:{qb}", "GD")
                    w.wait("t", "GD", w.ev[f"repack:{qb}"][1])
                    w.use("t", f"bank{ab}")
                    for hp2 in range(8):
                        w.op("t", lambda t, hp2=hp2, qt=qt, eh=eh, ab=ab: t.matmul(
                            bank(ab), ctx_sb[:, hp2, qt * 128:(qt + 1) * 128],
                            wout_sb[:, hp2, eh * 512:(eh + 1) * 512],
                            start=(hp2 == 0), stop=(hp2 == 7)), inc="T" if hp2 == 7 else None)
                    w.mark(f"opj:{qb}:{qt * 2 + eh}", "T")
                    w.wait_ev("v", f"opj:{qb}:{qt * 2 + eh}")
                    w.wait("v", "QD", w.ev[f"xo_ld:{qb}:{qt}"][1])
                    w.op("v", lambda v, qb=qb, qt=qt, eh=eh, ab=ab, xsl=xsl: v.tensor_add(
                        x2_sb[:, (qb * 4 + qt) * KB + eh * 512:
                              (qb * 4 + qt) * KB + (eh + 1) * 512],
                        bank(ab), xo_sb[xsl][:, eh * 512:(eh + 1) * 512]), inc="V")
                    w.mark(f"x2:{qb}:{qt * 2 + eh}", "V")
                    w.release(f"bank{ab}", f"x2:{qb}:{qt * 2 + eh}")
                    if eh == 1:
                        w.release(f"xo{xsl}", f"x2:{qb}:{qt * 2 + eh}")
                    if idx == 7:
                        w.release("ctx_sb", f"opj:{qb}:7")

            for idx in range(8):
                pending_opj.append(lambda idx=idx, qb=qb: opj_chain(idx, qb))
            if qb == 3:
                while pending_epi:
                    pending_epi.pop(0)()
                while pending_opj:
                    pending_opj.pop(0)()
        for sem_name, key in [("T", "ph2_T"), ("V", "ph2_V"), ("Sc", "ph2_S"),
                              ("QD", "ph2_QD"), ("GD", "ph2_GD")]:
            w.mark(key, sem_name)
        for eng in ["t", "v", "s", "q", "g"]:
            for key in ["ph2_T", "ph2_V", "ph2_S", "ph2_QD", "ph2_GD"]:
                w.wait_ev(eng, key)

        if debug:
            w.op("q", lambda q: q.dma_start(out=dbg_ctx, in_=ctx_sb[:, :, :]), inc="SO", by=16)
            w.mark("dbg_ctx", "SO")
            w.op("q", lambda q: q.dma_start(out=dbg_x2, in_=x2_sb[:]), inc="QD", by=16)
            w.mark("dbg_x2", "QD")

        wd0, off0 = EXP_W[0], EXP_OFF[0]
        w.use("g", "w1_sb")
        w.op("g", lambda g: g.dma_start(
            out=w1_sb[:, :, 0:wd0],
            in_=w1_in.rearrange("(c p) wdim -> p c wdim", p=128)[:, :, off0:off0 + wd0]),
            inc="GD", by=16)
        w.mark("w1_ld:0", "GD")
        w.use("g", "w2_sb")
        w.op("g", lambda g: g.dma_start(
            out=w2_sb[:, 0:wd0 // 128, :],
            in_=w2_in.rearrange("(wc p) d -> p wc d", p=128)[:, 0:wd0 // 128, :]),
            inc="GD", by=16)
        w.mark("w2_ld:0", "GD")

        for tt in range(16):
            sl = tt % 3
            src = x2_sb[:, tt * KB:(tt + 1) * KB]
            w.wait_ev("v", f"apply2:{tt - 3}")
            w.op("v", lambda v, sl=sl, src=src: v.bn_stats(bn6[:, sl, 0:6], src[:, 0:512]))
            w.op("v", lambda v, sl=sl, src=src: v.bn_stats(bn6[:, sl, 6:12], src[:, 512:1024]))
            w.vdrain()
            w.op("v", lambda v, sl=sl: v.bn_aggr(mv_sb[:, sl, :], bn6[:, sl, :]), inc="V")
            w.mark(f"mv2:{tt}", "V")
            w.wait_ev("s", f"mv2:{tt}")
            w.op("s", lambda s, sl=sl: s.activation(rstd4[:, sl:sl + 1], mv_sb[:, sl, 1:2],
                                                    AF.Ln, bias=eps_t[:, 0:1]))
            w.sdrain()
            w.op("s", lambda s, sl=sl: s.activation(rstd4[:, sl:sl + 1], rstd4[:, sl:sl + 1],
                                                    AF.Exp, scale=-0.5), inc="Sc")
            w.mark(f"rstd2:{tt}", "Sc")
            w.wait_ev("v", f"rstd2:{tt}")
            w.vdrain()
            w.op("v", lambda v, sl=sl: v.scalar_tensor_tensor(
                nmr4[:, sl:sl + 1], mv_sb[:, sl, 0:1], -1.0, rstd4[:, sl:sl + 1],
                ALU.mult, ALU.mult), inc="V")
            w.mark(f"nmr2:{tt}", "V")
            w.wait_ev("s", f"nmr2:{tt}")
            w.use("s", f"xn2tmp{sl}")
            w.op("s", lambda s, sl=sl, src=src: s.activation(xn2tmp[sl][:], src, AF.Identity,
                                                    bias=nmr4[:, sl:sl + 1],
                                                    scale=rstd4[:, sl:sl + 1]), inc="Sc")
            w.mark(f"apply2:{tt}", "Sc")
            w.wait_ev("t", f"apply2:{tt}")
            w.wait_ev("t", f"stage_f32:{tt - 3}")
            for c in range(8):
                b = 2 * sl + c // 4
                if c % 4 == 0:
                    w.use("t", f"bank{b}")
                off = (c % 4) * 128
                w.op("t", lambda t, c=c, b=b, off=off, sl=sl: t.matmul(
                    pall[:, b, off:off + 128], xn2tmp[sl][:, c * 128:(c + 1) * 128],
                    ident_f32[:], start=(c % 4 == 0), stop=(c % 4 == 3),
                    is_transpose=True), inc="T" if c == 7 else None)
            w.mark(f"tr2:{tt}", "T")
            w.release(f"xn2tmp{sl}", f"tr2:{tt}")
            w.wait_ev("v", f"tr2:{tt}")
            w.use("v", f"stage{sl}")
            psrc = pall[:, 2 * sl:2 * sl + 2, :].rearrange("p a b -> p (a b)").rearrange(
                "p (c q) -> p c q", c=8)
            w.op("v", lambda v, psrc=psrc, sl=sl: v.tensor_copy(stage[sl][:, :, :], psrc),
                 inc="V")
            w.mark(f"stage_f32:{tt}", "V")
            w.op("v", lambda v, psrc=psrc, tt=tt: v.tensor_copy(
                xn2_fm[:, :, tt * 128:(tt + 1) * 128], psrc), inc="V")
            w.mark(f"xn2cast:{tt}", "V")
            w.release(f"bank{2 * sl}", f"xn2cast:{tt}")
            w.release(f"bank{2 * sl + 1}", f"xn2cast:{tt}")
            w.wait_ev("t", f"stage_f32:{tt}")
            w.wait("t", "GD", w.ev["router_ld"][1])
            if tt == 0:
                w.use("t", "bank6")
            for c in range(8):
                w.op("t", lambda t, c=c, sl=sl, tt=tt: t.matmul(
                    pall[:, 6, tt * 8:(tt + 1) * 8], stage[sl][:, c, :], router_sb[:, c, :],
                    start=(c == 0 and tt == 0), stop=(c == 7 and tt == 15)),
                    inc="T" if c == 7 else None)
            w.mark(f"router:{tt}", "T")
            w.release(f"stage{sl}", f"router:{tt}")
        w.wait_ev("s", "router:15")
        if w.t == "s":
            from concourse.hw_specs import get_activation_tables
            sidx = list(get_activation_tables(nc.m.arch)).index("sigmoid_and_others")
            w.h["s"].add_instruction(mybir.InstLoadActFuncSet(
                name=nc.get_next_instruction_name(), act_func_set_id=sidx, ins=[], outs=[]))
        w.op("s", lambda s: s.activation(probs_sb[:], pall[:, 6, 0:128], AF.Sigmoid), inc="Sc")
        w.mark("sigmoid", "Sc")
        w.release("bank6", "sigmoid")
        for tt in range(16):
            pr = probs_sb[:, tt * 8:(tt + 1) * 8]
            w.wait_ev("v", "sigmoid")
            w.op("v", lambda v, pr=pr: v.reduce_max(m1c[:], pr, axis=AX.X), inc="V")
            w.selfwait("v", "V")
            w.op("v", lambda v, pr=pr: v.tensor_single_scalar(msk1_t[:], pr, m1c[:, 0:1],
                                                              ALU.is_equal))
            w.vdrain()
            w.op("v", lambda v, pr=pr: v.scalar_tensor_tensor(p2_t[:], msk1_t[:], -2.0, pr,
                                                              ALU.mult, ALU.add), inc="V")
            w.selfwait("v", "V")
            w.op("v", lambda v: v.reduce_max(m2c[:], p2_t[:], axis=AX.X), inc="V")
            w.selfwait("v", "V")
            w.op("v", lambda v: v.tensor_single_scalar(p2_t[:], p2_t[:], m2c[:, 0:1],
                                                       ALU.is_equal))
            w.vdrain()
            w.op("v", lambda v: v.tensor_add(msk1_t[:], msk1_t[:], p2_t[:]))
            w.vdrain()
            w.vdrain()
            w.op("v", lambda v: v.scalar_tensor_tensor(swc[:], m1c[:], 1e-20, m2c[:],
                                                       ALU.add, ALU.add), inc="V")
            w.selfwait("v", "V")
            w.op("v", lambda v: v.reciprocal(rcc[:], swc[:]), inc="V")
            w.selfwait("v", "V")
            w.vdrain()
            w.op("v", lambda v, pr=pr, tt=tt: v.scalar_tensor_tensor(
                c_sb[:, tt * 8:(tt + 1) * 8], msk1_t[:], rcc[:, 0:1], pr, ALU.mult, ALU.mult),
                inc="V")
            w.mark(f"c:{tt}", "V")
        for sem_name, key in [("T", "ph4_T"), ("V", "ph4_V"), ("Sc", "ph4_S")]:
            w.mark(key, sem_name)
        for eng in ["t", "v", "s", "q", "g"]:
            for key in ["ph4_T", "ph4_V", "ph4_S"]:
                w.wait_ev(eng, key)

        if debug:
            w.op("q", lambda q: q.dma_start(out=dbg_s, in_=cap_s[:]), inc="QD", by=16)
            w.op("q", lambda q: q.dma_start(out=dbg_p, in_=cap_p[:, :, :]), inc="QD", by=16)
            w.op("q", lambda q: q.dma_start(out=dbg_xn2, in_=xn2_fm[:, :, :]), inc="QD", by=16)
            w.op("q", lambda q: q.dma_start(out=dbg_probs, in_=probs_sb[:]), inc="QD", by=16)
            w.op("q", lambda q: q.dma_start(out=dbg_c, in_=c_sb[:]), inc="QD", by=16)
            for i in range(2):
                w.op("q", lambda q, i=i: q.dma_start(out=dbg_tmp[i], in_=xn2tmp[i][:, :]),
                     inc="QD", by=16)
                w.op("q", lambda q, i=i: q.dma_start(out=dbg_stage[i], in_=stage[i][:, :, :]),
                     inc="QD", by=16)
            w.mark("dbg_ph4", "QD")
            w.wait_ev("v", "dbg_x2")
            w.wait_ev("v", "dbg_ph4")
            w.wait_ev("g", "dbg_ph4")
            w.wait_ev("g", "dbg_ctx")

        for e in range(8):
            wd = EXP_W[e]
            off = EXP_OFF[e]
            nwc = wd // 128
            if e == 0:
                continue_loads = False
            else:
                continue_loads = True
            if continue_loads:
                w.use("g", "w1_sb")
                w.op("g", lambda g, off=off, wd=wd: g.dma_start(
                    out=w1_sb[:, :, 0:wd],
                    in_=w1_in.rearrange("(c p) wdim -> p c wdim", p=128)[:, :, off:off + wd]),
                    inc="GD", by=16)
                w.mark(f"w1_ld:{e}", "GD")
                w.use("g", "w2_sb")
                w.op("g", lambda g, off=off, nwc=nwc: g.dma_start(
                    out=w2_sb[:, 0:nwc, :],
                    in_=w2_in.rearrange("(wc p) d -> p wc d", p=128)[:, off // 128: off // 128 + nwc, :]),
                    inc="GD", by=16)
                w.mark(f"w2_ld:{e}", "GD")
            for tb in range(4):
                for wc in range(nwc):
                    hb = wc % 2
                    w.wait("t", "GD", w.ev[f"w1_ld:{e}"][1])
                    w.use("t", f"bank{hb}")
                    for c in range(8):
                        w.op("t", lambda t, c=c, wc=wc, tb=tb, hb=hb: t.matmul(
                            bank(hb), w1_sb[:, c, wc * 128:(wc + 1) * 128],
                            xn2_fm[:, c, tb * 512:(tb + 1) * 512],
                            start=(c == 0), stop=(c == 7)), inc="T" if c == 7 else None)
                    w.mark(f"hmm:{e}:{tb}:{wc}", "T")
                    if tb == 3 and wc == nwc - 1:
                        w.release("w1_sb", f"hmm:{e}:{tb}:{wc}")
                    w.wait_ev("s", f"hmm:{e}:{tb}:{wc}")
                    w.use("s", f"h1{hb}")
                    w.op("s", lambda s, hb=hb: s.activation(h1_s[hb][:], bank(hb), AF.Relu),
                         inc="Sc")
                    w.mark(f"relu:{e}:{tb}:{wc}", "Sc")
                    w.release(f"bank{hb}", f"relu:{e}:{tb}:{wc}")
                    w.wait_ev("v", f"relu:{e}:{tb}:{wc}")
                    if wc == 0:
                        w.use("v", "h2_sb")
                    w.op("v", lambda v, wc=wc, hb=hb: v.tensor_mul(
                        h2_sb[:, wc, :], h1_s[hb][:], h1_s[hb][:]), inc="V")
                    w.mark(f"h2sq:{e}:{tb}:{wc}", "V")
                    w.release(f"h1{hb}", f"h2sq:{e}:{tb}:{wc}")
                for tt in range(4):
                    ttg = tb * 4 + tt
                    for dh in range(2):
                        yb = 2 + (tt * 2 + dh) % 2
                        w.wait_ev("t", f"h2sq:{e}:{tb}:{nwc - 1}")
                        w.wait("t", "GD", w.ev[f"w2_ld:{e}"][1])
                        w.use("t", f"bank{yb}")
                        for wc in range(nwc):
                            w.op("t", lambda t, wc=wc, tt=tt, dh=dh, yb=yb, nwc=nwc: t.matmul(
                                bank(yb), h2_sb[:, wc, tt * 128:(tt + 1) * 128],
                                w2_sb[:, wc, dh * 512:(dh + 1) * 512],
                                start=(wc == 0), stop=(wc == nwc - 1)),
                                inc="T" if wc == nwc - 1 else None)
                        w.mark(f"ymm:{e}:{tb}:{tt * 2 + dh}", "T")
                        if tb == 3 and tt == 3 and dh == 1:
                            w.release("w2_sb", f"ymm:{e}:{tb}:{tt * 2 + dh}")
                        w.wait_ev("v", f"ymm:{e}:{tb}:{tt * 2 + dh}")
                        w.op("v", lambda v, e=e, ttg=ttg, dh=dh, yb=yb: v.scalar_tensor_tensor(
                            x2_sb[:, ttg * KB + dh * 512: ttg * KB + (dh + 1) * 512],
                            bank(yb), c_sb[:, ttg * 8 + e: ttg * 8 + e + 1],
                            x2_sb[:, ttg * KB + dh * 512: ttg * KB + (dh + 1) * 512],
                            ALU.mult, ALU.add), inc="V")
                        w.mark(f"ysc:{e}:{ttg}:{dh}", "V")
                        w.release(f"bank{yb}", f"ysc:{e}:{ttg}:{dh}")
                w.release("h2_sb", f"ymm:{e}:{tb}:7")

        for ttg in range(16):
            w.wait_ev("q", f"ysc:7:{ttg}:1")
            w.op("q", lambda q, ttg=ttg: q.dma_start(
                out=out_d[ttg * 128:(ttg + 1) * 128, :],
                in_=x2_sb[:, ttg * KB:(ttg + 1) * KB]), inc="SO", by=16)
        w.wait("q", "SO", 16 * 16)

    @blk.sync
    def _(q):
        walk(Walk("q", {"q": q}, sems))

    @blk.gpsimd
    def _(g):
        walk(Walk("g", {"g": g}, sems))

    @blk.vector
    def _(v):
        walk(Walk("v", {"v": v}, sems))

    @blk.scalar
    def _(s):
        walk(Walk("s", {"s": s}, sems))

    @blk.tensor
    def _(t):
        walk(Walk("t", {"t": t}, sems))

    es.close()
    return nc


def _stage_inputs(L0_attn_x, L0_attn_w_qkv, L0_attn_w_out, L0_ffn_router_w,
                  L0_ffn_w1, L0_ffn_w2):
    x = np.ascontiguousarray(np.asarray(L0_attn_x, dtype=np.float32))
    w_qkv = np.asarray(L0_attn_w_qkv, dtype=np.float32)
    w_out = np.asarray(L0_attn_w_out, dtype=np.float32)
    router_w = np.asarray(L0_ffn_router_w, dtype=np.float32)
    w1 = np.ascontiguousarray(np.asarray(L0_ffn_w1, dtype=np.float32))
    w2 = np.ascontiguousarray(np.asarray(L0_ffn_w2, dtype=np.float32))

    wq = w_qkv[:, :, 0:64].reshape(D, 1024)
    wk = w_qkv[:, :, 64:128].reshape(D, 1024)
    wv = w_qkv[:, :, 128:192].reshape(D, 1024)
    wqkv = np.concatenate([wq, wk, wv], axis=1)
    wqkv_st = np.ascontiguousarray(
        wqkv.reshape(8, 128, 3072).transpose(1, 0, 2).reshape(128, 8 * 3072))
    wout_st = np.ascontiguousarray(
        w_out.reshape(8, 2, 64, KB).transpose(1, 2, 0, 3).reshape(128, 8 * KB))
    router_st = np.ascontiguousarray(
        router_w.T.reshape(8, 128, 8).transpose(1, 0, 2).reshape(128, 64))
    ident_bf = np.eye(128, dtype=ml_dtypes.bfloat16)
    ident_f32 = np.eye(128, dtype=np.float32)

    masks = {}
    ki = np.arange(128)[:, None]
    qj = np.arange(512)[None, :]
    for r in (0, 1):
        m = np.zeros((128, 4, 8, 512), dtype=ml_dtypes.bfloat16)
        for i in range(4):
            q0 = QBG[r][i] * 512
            for sl in range(8):
                ktg = TKS[i] - 8 + sl
                k0 = ktg * 128
                m[:, i, sl, :] = ((k0 + ki) <= (q0 + qj)).astype(ml_dtypes.bfloat16)
        masks[r] = m

    in_maps, row_maps = [], []
    for c in range(8):
        b, r = c // 2, c % 2
        rows = np.concatenate([g * 512 + np.arange(512) for g in QBG[r]])
        row_maps.append((b, rows))
        in_maps.append({
            "x_keys": x[b],
            "x_own": np.ascontiguousarray(x[b][rows]),
            "wqkv_in": wqkv_st,
            "wout_in": wout_st,
            "router_in": router_st,
            "w1_in": w1,
            "w2_in": w2,
            "masks_in": masks[r],
            "ident_bf_in": ident_bf,
            "ident_f32_in": ident_f32,
        })
    return in_maps, row_maps, x.shape


def kernel(L0_attn_x, L0_attn_w_qkv, L0_attn_w_out, L0_ffn_router_w,
           L0_ffn_w1, L0_ffn_w2):
    in_maps, row_maps, xshape = _stage_inputs(
        L0_attn_x, L0_attn_w_qkv, L0_attn_w_out, L0_ffn_router_w, L0_ffn_w1, L0_ffn_w2)
    nc = build_nc()
    res = run_bass_kernel_spmd(nc, in_maps, core_ids=list(range(8)))
    out = np.empty(xshape, dtype=np.float32)
    for c in range(8):
        b, rows = row_maps[c]
        out[b][rows] = np.asarray(res.results[c]["out"], dtype=np.float32)
    return out


# version 20
# speedup vs baseline: 1.0155x; 1.0034x over previous
import numpy as np
import ml_dtypes
import concourse.bass as bass
import concourse.mybir as mybir
from concourse.bass_utils import run_bass_kernel_spmd

F32 = mybir.dt.float32
BF16 = mybir.dt.bfloat16
AF = mybir.ActivationFunctionType
ALU = mybir.AluOpType
AX = mybir.AxisListType

S = 4096
D = 1024
H = 16
HP = 8
NKB = 8
TKS = [8, 16, 24, 32]
QBG = {0: [0, 3, 4, 7], 1: [1, 2, 5, 6]}
EXP_W = [1024] * 4 + [2048] * 4
EXP_OFF = [0, 1024, 2048, 3072, 4096, 6144, 8192, 10240]
RD = 1.0 / D
KB = 1024


class Walk:

    def __init__(self, target, handles, sems):
        self.t = target
        self.h = handles
        self.sems = sems
        self.cnt = {k: 0 for k in sems}
        self.ev = {}
        self.free = {}
        self.lastwait = {}

    def op(self, eng, fn, inc=None, by=1):
        if eng == self.t:
            ins = fn(self.h[eng])
            if inc is not None:
                ins.then_inc(self.sems[inc], by)
        if inc is not None:
            self.cnt[inc] += by

    def mark(self, key, sem):
        self.ev[key] = (sem, self.cnt[sem])

    def wait_ev(self, eng, key):
        if key not in self.ev:
            return
        sem, val = self.ev[key]
        self.wait(eng, sem, val)

    def wait(self, eng, sem, val):
        if val <= 0 or eng != self.t:
            return
        if self.lastwait.get(sem, 0) >= val:
            return
        self.h[eng].wait_ge(self.sems[sem], val)
        self.lastwait[sem] = val

    def selfwait(self, eng, sem):
        self.wait(eng, sem, self.cnt[sem])

    def vdrain(self):
        if self.t == "v":
            self.h["v"].drain()

    def sdrain(self):
        if self.t == "s":
            self.h["s"].drain()

    def use(self, eng, res):
        key = self.free.get(res)
        if key is not None:
            self.wait_ev(eng, key)

    def release(self, res, key):
        self.free[res] = key


def build_nc(debug=False):
    nc = bass.Bass("TRN2", target_bir_lowering=False, debug=False)

    x_keys = nc.dram_tensor("x_keys", [S, D], F32, kind="ExternalInput").ap()
    x_own = nc.dram_tensor("x_own", [2048, D], F32, kind="ExternalInput").ap()
    wqkv_in = nc.dram_tensor("wqkv_in", [128, 8 * 3072], F32, kind="ExternalInput").ap()
    wout_in = nc.dram_tensor("wout_in", [128, 8 * KB], F32, kind="ExternalInput").ap()
    router_in = nc.dram_tensor("router_in", [128, 64], F32, kind="ExternalInput").ap()
    w1_in = nc.dram_tensor("w1_in", [D, 12288], F32, kind="ExternalInput").ap()
    w2_in = nc.dram_tensor("w2_in", [12288, D], F32, kind="ExternalInput").ap()
    masks_in = nc.dram_tensor("masks_in", [128, 4, 8, 512], BF16, kind="ExternalInput").ap()
    ident_bf_in = nc.dram_tensor("ident_bf_in", [128, 128], BF16, kind="ExternalInput").ap()
    ident_f32_in = nc.dram_tensor("ident_f32_in", [128, 128], F32, kind="ExternalInput").ap()
    out_d = nc.dram_tensor("out", [2048, D], F32, kind="ExternalOutput").ap()
    scr_kind = "ExternalOutput" if debug else "Internal"
    k_scr = nc.dram_tensor("k_scr", [8, NKB, 128, 512], BF16, kind=scr_kind).ap()
    v_scr = nc.dram_tensor("v_scr", [8, NKB, 4, 128, 196], BF16, kind=scr_kind).ap()
    if debug:
        dbg_q = nc.dram_tensor("dbg_q", [128, 8, 2048], BF16, kind="ExternalOutput").ap()
        dbg_ctx = nc.dram_tensor("dbg_ctx", [64, 16, 512], BF16, kind="ExternalOutput").ap()
        dbg_x2 = nc.dram_tensor("dbg_x2", [128, 16 * KB], F32, kind="ExternalOutput").ap()
        dbg_xn2 = nc.dram_tensor("dbg_xn2", [128, 8, 2048], BF16, kind="ExternalOutput").ap()
        dbg_probs = nc.dram_tensor("dbg_probs", [128, 128], F32, kind="ExternalOutput").ap()
        dbg_c = nc.dram_tensor("dbg_c", [128, 128], F32, kind="ExternalOutput").ap()
        dbg_stats = nc.dram_tensor("dbg_stats", [6, 128, 48], F32, kind="ExternalOutput").ap()
        dbg_tmp = nc.dram_tensor("dbg_tmp", [2, 128, KB], F32, kind="ExternalOutput").ap()
        dbg_stage = nc.dram_tensor("dbg_stage", [2, 128, 8, 128], F32, kind="ExternalOutput").ap()
        dbg_s = nc.dram_tensor("dbg_s", [128, 1024], F32, kind="ExternalOutput").ap()
        dbg_p = nc.dram_tensor("dbg_p", [128, 2, 512], BF16, kind="ExternalOutput").ap()

    SB_BASE = 16512

    def sb(name, off, shape, dtype):
        return nc.alloc_sbuf_tensor_at(name, shape, dtype, offset=SB_BASE + off).ap()

    ident_bf = sb("ident_bf", 0, [128, 128], BF16)
    ident_f32 = sb("ident_f32", 256, [128, 128], F32)
    eps_t = sb("eps_t", 768, [128, 8], F32)
    ones64f = sb("ones64f", 800, [128, 64], F32)
    router_sb = sb("router_sb", 1056, [128, 8, 8], F32)
    bn6 = sb("bn6", 1312, [128, 8, 12], F32)
    mv_sb = sb("mv_sb", 1728, [128, 8, 2], F32)
    rstd4 = sb("rstd4", 1824, [128, 8], F32)
    nmr4 = sb("nmr4", 1856, [128, 8], F32)
    probs_sb = sb("probs_sb", 2464, [128, 128], F32)
    c_sb = sb("c_sb", 2976, [128, 128], F32)
    msk1_t = sb("msk1_t", 3488, [128, 8], F32)
    p2_t = sb("p2_t", 3520, [128, 8], F32)
    m1c = sb("m1c", 3552, [128, 1], F32)
    m2c = sb("m2c", 3584, [128, 1], F32)
    swc = sb("swc", 3616, [128, 1], F32)
    rcc = sb("rcc", 3648, [128, 1], F32)

    wqkv_sb = sb("wqkv_sb", 4 * KB, [128, 8, 3072], BF16)
    x2_sb = sb("x2_sb", 4 * KB, [128, 16 * KB], F32)

    q_fm = sb("q_fm", 68 * KB, [128, 8, 2048], BF16)
    xn2_fm = sb("xn2_fm", 68 * KB, [128, 8, 2048], BF16)

    R = 100 * KB
    x_in = [sb(f"x_in{i}", R + i * 16 * KB, [128, 4, KB], F32) for i in range(2)]
    x_in.append(sb("x_in2", 52 * KB, [128, 4, KB], F32))
    xn_tm = [sb(f"xn_tm{i}", R + 32 * KB + i * 2 * KB, [128, KB], BF16) for i in range(2)]
    xn_fm = [sb(f"xn_fm{i}", R + 36 * KB + i * 8 * KB, [128, 8, 512], BF16) for i in range(3)]
    kstg = [sb(f"kstg{i}", R + 60 * KB + i * 8 * KB, [128, 8, 512], BF16) for i in range(2)]
    vstg = [sb(f"vstg{i}", R + 76 * KB + i * 12544, [128, 4, 8, 196], BF16) for i in range(2)]
    _c = [R]

    def _nxt(nbytes, align=64):
        off = (_c[0] + align - 1) // align * align
        _c[0] = off + nbytes
        return off

    kst = [sb(f"kst{i}", _nxt(1024), [128, 512], BF16) for i in range(6)]
    vst = [sb(f"vst{i}", _nxt(1568), [128, 4, 196], BF16) for i in range(6)]
    p_sb = [sb(f"p_sb{i}", _nxt(2048), [128, 2, 512], BF16) for i in range(3)]
    mask_sb = sb("mask_sb", _nxt(8192), [128, 8, 512], BF16)
    recip_sb = sb("recip_sb", _nxt(4096), [128, 2, 512], F32)
    recipb_sb = [sb(f"recipb_sb{i}", _nxt(2048), [64, 512], F32) for i in range(2)]
    xo_sb = [sb(f"xo_sb{i}", _nxt(4096), [128, KB], F32) for i in range(3)]
    ctx_sb = sb("ctx_sb", _nxt(8192), [128, 8, 512], BF16)
    ctxo_sb = sb("ctxo_sb", _nxt(8192), [64, 8, 512], BF16)
    wout_sb = sb("wout_sb", _nxt(16384), [128, 8, KB], BF16)
    assert _c[0] - R <= 99 * KB, _c[0] - R
    xn2tmp = [sb(f"xn2tmp{i}", R + 83 * KB + i * 4 * KB, [128, KB], F32) for i in range(3)]
    stage = [sb(f"stage{i}", R + 95 * KB + i * 4 * KB, [128, 8, 128], F32) for i in range(3)]
    w1_sb = sb("w1_sb", R, [128, 8, 2048], BF16)
    w2_sb = sb("w2_sb", R + 32 * KB, [128, 16, KB], BF16)
    h1_s = [sb(f"h1_s{i}", R + 64 * KB + i * KB, [128, 512], BF16) for i in range(2)]

    h2_sb = sb("h2_sb", R + 66 * KB, [128, 16, 512], BF16)

    cap_s = sb("cap_s", R + 94 * KB, [128, 1024], F32) if debug else None
    cap_p = sb("cap_p", R + 98 * KB, [128, 2, 512], BF16) if debug else None
    pall = nc.alloc_psum_tensor("pall", [128, 8, 512], F32).ap()

    def bank(i):
        return pall[:, i, :]

    import contextlib
    es = contextlib.ExitStack()
    sems = {}
    for name in ["T", "V", "Sc", "QD", "GD", "SO", "WQ", "RT", "XD0", "XD1", "XD2",
                 "SP", "KV0", "KV1", "KV2", "KV3", "KV4", "KV5", "MS",
                 "XO0", "XO1", "XO2", "WO", "RP", "W1", "W2"]:
        sems[name] = es.enter_context(nc.semaphore(name))
    blk = es.enter_context(nc.Block())

    def walk(w: Walk):
        w.op("g", lambda g: g.dma_start(out=wqkv_sb[:, :, :],
                                        in_=wqkv_in.rearrange("p (c k) -> p c k", c=8)),
             inc="WQ", by=16)
        w.mark("wqkv_ld", "WQ")
        w.op("g", lambda g: g.dma_start(out=router_sb[:, :, :],
                                        in_=router_in.rearrange("p (c e) -> p c e", c=8)),
             inc="RT", by=16)
        w.mark("router_ld", "RT")
        w.op("q", lambda q: q.dma_start(out=ident_bf[:], in_=ident_bf_in), inc="QD", by=16)
        w.op("q", lambda q: q.dma_start(out=ident_f32[:], in_=ident_f32_in), inc="QD", by=16)
        w.mark("ident_ld", "QD")
        w.op("v", lambda v: v.memset(eps_t[:], 1e-5))
        w.op("v", lambda v: v.memset(ones64f[:], 1.0))
        w.op("v", lambda v: v.memset(vstg[0][:, :, :, 64:65], 1.0))
        w.op("v", lambda v: v.memset(vstg[1][:, :, :, 64:65], 1.0))
        w.op("v", lambda v: v.memset(vstg[0][:, :, :, 129:130], 1.0))
        w.op("v", lambda v: v.memset(vstg[1][:, :, :, 129:130], 1.0))
        w.op("v", lambda v: v.memset(vstg[0][:, :, :, 130:196], 0.0))
        w.op("v", lambda v: v.memset(vstg[1][:, :, :, 130:196], 0.0), inc="V")
        w.mark("consts_v", "V")
        if w.t == "s":
            from concourse.hw_specs import get_activation_tables
            idx = list(get_activation_tables(nc.m.arch)).index("natural_log_exp_and_others")
            w.h["s"].add_instruction(mybir.InstLoadActFuncSet(
                name=nc.get_next_instruction_name(), act_func_set_id=idx, ins=[], outs=[]))

        UNITS = []
        for _qb in range(4):
            for _hp in range(HP):
                for _kb2 in range(TKS[_qb] // 4):
                    UNITS.append((_hp, _kb2))

        def kv_load_g(u):
            sl2 = u % 6
            hp_u, kb2_u = UNITS[u]
            w.use("q", f"kst{sl2}")
            w.use("q", f"vst{sl2}")
            w.op("q", lambda q, hp_u=hp_u, kb2_u=kb2_u, sl2=sl2: q.dma_start(
                out=kst[sl2][:, :], in_=k_scr[hp_u, kb2_u]), inc=f"KV{sl2}", by=16)
            w.op("q", lambda q, hp_u=hp_u, kb2_u=kb2_u, sl2=sl2: q.dma_start(
                out=vst[sl2][:, :, :],
                in_=v_scr[hp_u, kb2_u].rearrange("a p b -> p a b")), inc=f"KV{sl2}", by=16)
            w.mark(f"kv_ld:{u}", f"KV{sl2}")

        def ln_stats_block(jbase, tiles, xin_ev):
            nt = len(tiles)
            c0 = ((jbase // 4) % 2) * 4
            w.wait_ev("v", xin_ev)
            w.wait_ev("v", f"lnapply_done:{jbase - 8}")
            for t, src_ap in enumerate(tiles):
                w.op("v", lambda v, t=c0 + t, src_ap=src_ap: v.bn_stats(
                    bn6[:, t, 0:6], src_ap[:, 0:512]))
                w.op("v", lambda v, t=c0 + t, src_ap=src_ap: v.bn_stats(
                    bn6[:, t, 6:12], src_ap[:, 512:1024]))
            w.vdrain()
            for t in range(nt):
                w.op("v", lambda v, t=c0 + t: v.bn_aggr(mv_sb[:, t, :], bn6[:, t, :]),
                     inc="V" if t == nt - 1 else None)
            w.mark(f"mv:{jbase}", "V")
            w.wait_ev("s", f"mv:{jbase}")
            mvv = mv_sb.rearrange("p a b -> p (a b)")
            w.op("s", lambda s, nt=nt, mvv=mvv, c0=c0: s.activation(
                rstd4[:, c0:c0 + nt], mvv[:, 2 * c0 + 1:2 * (c0 + nt):2],
                AF.Ln, bias=eps_t[:, 0:1]))
            w.sdrain()
            w.op("s", lambda s, nt=nt, c0=c0: s.activation(
                rstd4[:, c0:c0 + nt], rstd4[:, c0:c0 + nt], AF.Exp, scale=-0.5),
                inc="Sc")
            w.mark(f"rstdb:{jbase}", "Sc")
            w.wait_ev("v", f"rstdb:{jbase}")
            w.vdrain()
            w.op("v", lambda v, nt=nt, mvv=mvv, c0=c0: v.scalar_tensor_tensor(
                nmr4[:, c0:c0 + nt], mvv[:, 2 * c0:2 * (c0 + nt):2], -1.0,
                rstd4[:, c0:c0 + nt], ALU.mult, ALU.mult), inc="V")
            w.mark(f"nmrb:{jbase}", "V")

        def ln_apply(j, t, src_ap, xnslot, jbase):
            c0 = ((jbase // 4) % 2) * 4
            w.wait_ev("s", f"nmrb:{jbase}")
            w.use("s", f"xn_tm{xnslot}")
            w.op("s", lambda s, t=c0 + t, src_ap=src_ap, xnslot=xnslot: s.activation(
                xn_tm[xnslot][:], src_ap, AF.Identity,
                bias=nmr4[:, t:t + 1], scale=rstd4[:, t:t + 1]), inc="Sc")
            w.mark(f"apply:{j}", "Sc")

        def tr_evac(j, xnslot, bsl, fmslot, tb):
            trv = pall[:, bsl, :].bitcast(BF16).rearrange("p (c q) -> p c q", c=8)
            w.wait_ev("t", f"apply:{j}")
            w.use("t", f"bank{bsl}")
            w.wait_ev("t", "ident_ld")
            for c in range(8):
                w.op("t", lambda t, c=c: t.matmul(trv[:, c, :],
                                                  xn_tm[xnslot][:, c * 128:(c + 1) * 128],
                                                  ident_bf[:], start=(c == 0), stop=(c == 7),
                                                  is_transpose=True),
                     inc="T" if c == 7 else None)
            w.mark(f"tr:{j}", "T")
            w.release(f"xn_tm{xnslot}", f"tr:{j}")
            w.wait_ev("v", f"tr:{j}")
            w.use("v", f"xn_fm{fmslot}")
            w.op("v", lambda v: v.tensor_copy(xn_fm[fmslot][:, :, tb * 128:(tb + 1) * 128],
                                              trv[:, :, :]), inc="V")
            w.mark(f"xnfm:{j}", "V")
            w.release(f"bank{bsl}", f"xnfm:{j}")

        def p1_front(blk_i, is_keys):
            base_j = blk_i * 4
            src_dram = x_keys if is_keys else x_own
            row0 = blk_i * 512 if is_keys else (blk_i - NKB) * 512
            fmslot = blk_i % 3
            xsl = blk_i % 2 if blk_i < NKB else {8: 2, 9: 0, 10: 1, 11: 2}[blk_i]
            w.use("g", f"x_in{xsl}")
            w.op("g", lambda g, row0=row0, xsl=xsl: g.dma_start(
                out=x_in[xsl][:, :, :],
                in_=src_dram[row0: row0 + 512, :]
                .rearrange("(a p) d -> p a d", p=128)), inc=f"XD{xsl}", by=16)
            w.mark(f"xin:{blk_i}", f"XD{xsl}")
            ln_stats_block(base_j, [x_in[xsl][:, t, :] for t in range(4)],
                           f"xin:{blk_i}")
            for t2 in range(4):
                j = base_j + t2
                ln_apply(j, t2, x_in[xsl][:, t2, :], j % 2, base_j)
                if t2 == 3:
                    w.release(f"x_in{xsl}", f"apply:{j}")
                    w.mark(f"lnapply_done:{base_j}", "Sc")
                tr_evac(j, j % 2, j % 2, fmslot, t2)

        def p1_back(blk_i, is_keys):
            base_j = blk_i * 4
            fmslot = blk_i % 3
            ksl = blk_i % 2
            w.wait_ev("t", f"xnfm:{base_j + 3}")
            w.wait_ev("t", "wqkv_ld")
            if is_keys:
                for hp in range(HP):
                    kb = 2 + hp % 2
                    w.use("t", f"bank{kb}")
                    for c in range(8):
                        w.op("t", lambda t, c=c, hp=hp, kb=kb: t.matmul(
                            bank(kb), wqkv_sb[:, c, 1024 + hp * 128: 1024 + (hp + 1) * 128],
                            xn_fm[fmslot][:, c, :], start=(c == 0), stop=(c == 7)),
                            inc="T" if c == 7 else None)
                    w.mark(f"kproj:{blk_i}:{hp}", "T")
                    w.wait_ev("s", f"kproj:{blk_i}:{hp}")
                    w.use("s", f"kstg{ksl}")
                    w.op("s", lambda s, hp=hp, kb=kb, ksl=ksl: s.copy(
                        kstg[ksl][:, hp, :], bank(kb)), inc="Sc")
                    w.mark(f"kevac:{blk_i}:{hp}", "Sc")
                    w.release(f"bank{kb}", f"kevac:{blk_i}:{hp}")
                for kt in range(4):
                    b0 = 4 + 2 * (kt % 2)
                    w.use("t", f"bank{b0}")
                    w.use("t", f"bank{b0 + 1}")
                    for half2 in range(2):
                        for c in range(8):
                            w.op("t", lambda t, c=c, kt=kt, half2=half2, b0=b0: t.matmul(
                                bank(b0 + half2), xn_fm[fmslot][:, c, kt * 128:(kt + 1) * 128],
                                wqkv_sb[:, c, 2048 + half2 * 512: 2048 + (half2 + 1) * 512],
                                start=(c == 0), stop=(c == 7)),
                                inc="T" if (c == 7 and half2 == 1) else None)
                    w.mark(f"vproj:{blk_i}:{kt}", "T")
                    w.wait_ev("s", f"vproj:{blk_i}:{kt}")
                    w.use("s", f"vstg{ksl}")
                    w.wait("s", "V", w.ev["consts_v"][1])
                    vsrc = pall[:, b0:b0 + 2, :].rearrange("p a b -> p (a b)").rearrange(
                        "p (a b dd) -> p a b dd", a=8, b=2)
                    w.op("s", lambda s, kt=kt, vsrc=vsrc, ksl=ksl: s.copy(
                        vstg[ksl][:, kt, :, 0:64], vsrc[:, :, 0, :]))
                    w.op("s", lambda s, kt=kt, vsrc=vsrc, ksl=ksl: s.copy(
                        vstg[ksl][:, kt, :, 65:129], vsrc[:, :, 1, :]), inc="Sc")
                    w.mark(f"vevac:{blk_i}:{kt}", "Sc")
                    w.release(f"bank{b0}", f"vevac:{blk_i}:{kt}")
                    w.release(f"bank{b0 + 1}", f"vevac:{blk_i}:{kt}")
                for hp in range(HP):
                    w.wait_ev("q", f"kevac:{blk_i}:{hp}")
                    w.op("q", lambda q, hp=hp, blk_i=blk_i, ksl=ksl: q.dma_start(
                        out=k_scr[hp, blk_i], in_=kstg[ksl][:, hp, :]),
                        inc="SP", by=16)
                    w.wait_ev("q", f"vevac:{blk_i}:3")
                    w.op("q", lambda q, hp=hp, blk_i=blk_i, ksl=ksl: q.dma_start(
                        out=v_scr[hp, blk_i].rearrange("a p b -> p a b"),
                        in_=vstg[ksl][:, :, hp, :]), inc="SP", by=16)
                w.mark(f"spill:{blk_i}", "SP")
                w.release(f"kstg{ksl}", f"spill:{blk_i}")
                w.release(f"vstg{ksl}", f"spill:{blk_i}")
            else:
                ob = blk_i - NKB
                for hp in range(HP):
                    kb = 2 + hp % 2
                    w.use("t", f"bank{kb}")
                    for c in range(8):
                        w.op("t", lambda t, c=c, hp=hp, kb=kb: t.matmul(
                            bank(kb), wqkv_sb[:, c, hp * 128:(hp + 1) * 128],
                            xn_fm[fmslot][:, c, :], start=(c == 0), stop=(c == 7)),
                            inc="T" if c == 7 else None)
                    w.mark(f"qproj:{blk_i}:{hp}", "T")
                    w.wait_ev("s", f"qproj:{blk_i}:{hp}")
                    w.op("s", lambda s, hp=hp, kb=kb, ob=ob: s.copy(
                        q_fm[:, hp, ob * 512:(ob + 1) * 512], bank(kb)), inc="Sc")
                    w.mark(f"qevac:{blk_i}:{hp}", "Sc")
                    w.release(f"bank{kb}", f"qevac:{blk_i}:{hp}")
            w.mark(f"proj_done:{blk_i}", "T")
            w.release(f"xn_fm{fmslot}", f"proj_done:{blk_i}")

        for blk_i in range(3):
            p1_front(blk_i, blk_i < NKB)
        for blk_i in range(NKB + 4):
            p1_back(blk_i, blk_i < NKB)
            if blk_i + 3 < NKB + 4:
                p1_front(blk_i + 3, blk_i + 3 < NKB)
        w.wait_ev("q", "apply:39")
        for u in range(6):
            kv_load_g(u)
        PH1_SEMS = ["T", "V", "Sc", "QD", "GD", "WQ", "RT", "XD0", "XD1", "XD2", "SP"]
        for sem_name in PH1_SEMS:
            key = f"ph1_{sem_name}"
            w.mark(key, sem_name)

        for eng in ["t", "v", "s", "q", "g"]:
            for key in [f"ph1_{s}" for s in PH1_SEMS]:
                w.wait_ev(eng, key)

        if debug:
            w.op("q", lambda q: q.dma_start(out=dbg_q, in_=q_fm[:, :, :]), inc="QD", by=16)

        w.op("g", lambda g: g.dma_start(out=wout_sb[:, :, :],
                                        in_=wout_in.rearrange("p (hh e) -> p hh e", hh=8)),
             inc="WO", by=16)
        w.mark("wout_ld", "WO")

        ks_seq = 0
        pending_epi = []
        pending_opj = []
        for qb in range(4):
            TK = TKS[qb]
            w.use("q", "mask_sb")
            w.op("q", lambda q, qb=qb: q.dma_start(out=mask_sb[:, :, :],
                                                   in_=masks_in[:, qb, :, :]), inc="MS", by=16)
            w.mark(f"mask_ld:{qb}", "MS")

            for hp in range(HP):
                seq0 = ks_seq
                ks_seq += TK // 4
                nkb2 = TK // 4


                sstate = {}

                def s_part(ktg, hp=hp, qb=qb, TK=TK, seq0=seq0):
                    kb2, kt = divmod(ktg, 4)
                    sl2 = (seq0 + kb2) % 6
                    ss = sstate[ktg] = (seq0 * 4 + ktg) % 3
                    sp = (seq0 * 4 + ktg) % 2
                    sb0 = 2 * sp
                    w.wait_ev("t", f"kv_ld:{seq0 + kb2}")
                    w.use("t", f"bank{sb0}")
                    w.use("t", f"bank{sb0 + 1}")
                    for side in range(2):
                        w.op("t", lambda t, side=side, b=sb0 + side, sl2=sl2,
                             kt=kt, hp=hp, qb=qb: t.matmul(
                                 bank(b),
                                 kst[sl2][side * 64:(side + 1) * 64, kt * 128:(kt + 1) * 128],
                                 q_fm[side * 64:(side + 1) * 64, hp, qb * 512:(qb + 1) * 512],
                                 start=True, stop=True),
                             inc="T" if side == 1 else None)
                    w.mark(f"s:{qb}:{hp}:{ktg}", "T")
                    w.wait_ev("s", f"s:{qb}:{hp}:{ktg}")
                    w.use("s", f"p{ss}")
                    esrc = pall[:, sb0:sb0 + 2, :].rearrange("p a b -> p (a b)")
                    edst = p_sb[ss].rearrange("p a b -> p (a b)")
                    w.op("s", lambda s, esrc=esrc, edst=edst: s.activation(
                        edst, esrc, AF.Exp, scale=0.125), inc="Sc")
                    w.mark(f"exp:{qb}:{hp}:{ktg}", "Sc")
                    w.release(f"bank{sb0}", f"exp:{qb}:{hp}:{ktg}")
                    w.release(f"bank{sb0 + 1}", f"exp:{qb}:{hp}:{ktg}")
                    slot = ktg - (TK - 8)
                    if slot >= 0:
                        w.wait_ev("v", f"exp:{qb}:{hp}:{ktg}")
                        w.wait_ev("v", f"mask_ld:{qb}")
                        for hd in range(2):
                            w.op("v", lambda v, ss=ss, hd=hd, slot=slot: v.tensor_mul(
                                p_sb[ss][:, hd, :], p_sb[ss][:, hd, :],
                                mask_sb[:, slot, :]), inc="V" if hd == 1 else None)
                        w.mark(f"pmask:{qb}:{hp}:{ktg}", "V")

                def pv_part(ktg, hp=hp, qb=qb, TK=TK, seq0=seq0, nkb2=nkb2):
                    kb2, kt = divmod(ktg, 4)
                    sl2 = (seq0 + kb2) % 6
                    ss = sstate.pop(ktg)
                    if ktg - (TK - 8) >= 0:
                        w.wait_ev("t", f"pmask:{qb}:{hp}:{ktg}")
                    w.wait_ev("t", f"exp:{qb}:{hp}:{ktg}")
                    for hd in range(2):
                        if ktg == 0:
                            w.use("t", f"bank{6 + hd}")
                        w.op("t", lambda t, hd=hd, kt=kt, sl2=sl2, ss=ss,
                             ktg=ktg, TK=TK: t.matmul(
                                 pall[:, 6 + hd, :],
                                 vst[sl2][:, kt, hd * 65:hd * 65 + 128],
                                 p_sb[ss][:, hd, :],
                                 start=(ktg == 0), stop=(ktg == TK - 1)),
                             inc="T" if hd == 1 else None)
                    w.mark(f"pv:{qb}:{hp}:{ktg}", "T")
                    w.release(f"p{ss}", f"pv:{qb}:{hp}:{ktg}")
                    if kt == 3 or ktg == TK - 1:
                        w.release(f"kst{sl2}", f"pv:{qb}:{hp}:{ktg}")
                        w.release(f"vst{sl2}", f"pv:{qb}:{hp}:{ktg}")
                        if seq0 + kb2 + 6 < len(UNITS):
                            kv_load_g(seq0 + kb2 + 6)

                for k2 in range(0, TK, 2):
                    s_part(k2)
                    s_part(k2 + 1)
                    if k2 == 0 and pending_epi:
                        pending_epi.pop(0)()
                    if k2 >= 2 and pending_opj:
                        pending_opj.pop(0)()
                    if k2 >= 2:
                        pv_part(k2 - 2)
                        pv_part(k2 - 1)
                while pending_opj:
                    pending_opj.pop(0)()
                pv_part(TK - 2)
                pv_part(TK - 1)
                for hd in range(2):
                    cb = 6 + hd
                    w.wait_ev("s", f"pv:{qb}:{hp}:{TK - 1}")
                    w.use("s", f"recip{hd}")
                    w.op("s", lambda s, hd=hd, cb=cb: s.activation(
                        recip_sb[64:65, hd, :], pall[64:65, cb, :], AF.Ln))
                    w.sdrain()
                    w.op("s", lambda s, hd=hd: s.activation(
                        recip_sb[64:65, hd, :], recip_sb[64:65, hd, :], AF.Exp,
                        scale=-1.0), inc="Sc")
                    w.mark(f"recip:{qb}:{hp}:{hd}", "Sc")

                def epi_tv(qb=qb, hp=hp):
                    if hp == 0:
                        w.use("v", "ctx_sb")
                    for hd in range(2):
                        cb = 6 + hd
                        h = 2 * hp + hd
                        w.wait_ev("t", f"recip:{qb}:{hp}:{hd}")
                        w.use("t", f"bank{4 + hd}")
                        w.op("t", lambda t, hd=hd: t.matmul(
                            pall[0:64, 4 + hd, :], ones64f[64:65, 0:64],
                            recip_sb[64:65, hd, :], start=True, stop=True), inc="T")
                        w.mark(f"rb:{qb}:{hp}:{hd}", "T")
                        w.release(f"recip{hd}", f"rb:{qb}:{hp}:{hd}")
                        w.wait_ev("v", f"rb:{qb}:{hp}:{hd}")
                        w.use("v", f"recipb{hd}")
                        w.vdrain()
                        w.op("v", lambda v, hd=hd: v.tensor_copy(recipb_sb[hd][0:64, :],
                                                                 pall[0:64, 4 + hd, :]))
                        w.vdrain()
                        cdst = (ctx_sb[0:64, hp, :] if hd == 0 else ctxo_sb[0:64, hp, :])
                        w.op("v", lambda v, hd=hd, cb=cb, cdst=cdst: v.tensor_tensor(
                            cdst, pall[0:64, cb, :], recipb_sb[hd][0:64, :],
                            ALU.mult), inc="V")
                        w.mark(f"ctxevac:{qb}:{hp}:{hd}", "V")
                        w.release(f"recipb{hd}", f"ctxevac:{qb}:{hp}:{hd}")
                        w.release(f"bank{4 + hd}", f"ctxevac:{qb}:{hp}:{hd}")
                        w.release(f"bank{cb}", f"ctxevac:{qb}:{hp}:{hd}")
                    w.mark(f"ctx_done:{qb}:{hp}", "V")

                pending_epi.append(epi_tv)
            w.release("mask_sb", f"pmask:{qb}:7:{TK - 1}")
            def opj_chain(idx, qb=qb):
                qt, eh = divmod(idx, 2)
                if True:
                    ab = 4 + idx % 2
                    xsl = qt % 3
                    if eh == 0:
                        w.use("q", f"xo{xsl}")
                        w.op("q", lambda q, qb=qb, qt=qt, xsl=xsl: q.dma_start(
                            out=xo_sb[xsl][:, :],
                            in_=x_own[qb * 512 + qt * 128: qb * 512 + (qt + 1) * 128, :]),
                            inc=f"XO{xsl}", by=16)
                        w.mark(f"xo_ld:{qb}:{qt}", f"XO{xsl}")
                    if idx == 0:
                        w.wait("g", "V", w.ev[f"ctx_done:{qb}:7"][1])
                        w.op("g", lambda g: g.dma_start(out=ctx_sb[64:128, :, :],
                                                        in_=ctxo_sb[0:64, :, :]),
                             inc="RP", by=16)
                        w.mark(f"repack:{qb}", "RP")
                    w.wait_ev("t", f"repack:{qb}")
                    w.wait_ev("t", "wout_ld")
                    w.use("t", f"bank{ab}")
                    for hp2 in range(8):
                        w.op("t", lambda t, hp2=hp2, qt=qt, eh=eh, ab=ab: t.matmul(
                            bank(ab), ctx_sb[:, hp2, qt * 128:(qt + 1) * 128],
                            wout_sb[:, hp2, eh * 512:(eh + 1) * 512],
                            start=(hp2 == 0), stop=(hp2 == 7)), inc="T" if hp2 == 7 else None)
                    w.mark(f"opj:{qb}:{qt * 2 + eh}", "T")
                    w.wait_ev("v", f"opj:{qb}:{qt * 2 + eh}")
                    w.wait_ev("v", f"xo_ld:{qb}:{qt}")
                    w.op("v", lambda v, qb=qb, qt=qt, eh=eh, ab=ab, xsl=xsl: v.tensor_add(
                        x2_sb[:, (qb * 4 + qt) * KB + eh * 512:
                              (qb * 4 + qt) * KB + (eh + 1) * 512],
                        bank(ab), xo_sb[xsl][:, eh * 512:(eh + 1) * 512]), inc="V")
                    w.mark(f"x2:{qb}:{qt * 2 + eh}", "V")
                    w.release(f"bank{ab}", f"x2:{qb}:{qt * 2 + eh}")
                    if eh == 1:
                        w.release(f"xo{xsl}", f"x2:{qb}:{qt * 2 + eh}")
                    if idx == 7:
                        w.release("ctx_sb", f"opj:{qb}:7")

            for idx in range(8):
                pending_opj.append(lambda idx=idx, qb=qb: opj_chain(idx, qb))
            if qb == 3:
                while pending_epi:
                    pending_epi.pop(0)()
                while pending_opj:
                    pending_opj.pop(0)()
        PH2_SEMS = ["T", "V", "Sc", "QD", "GD", "KV0", "KV1", "KV2", "KV3", "KV4",
                    "KV5", "MS", "XO0", "XO1", "XO2", "WO", "RP"]
        for sem_name in PH2_SEMS:
            key = f"ph2_{sem_name}"
            w.mark(key, sem_name)
        for eng in ["t", "v", "s", "q", "g"]:
            for key in [f"ph2_{s}" for s in PH2_SEMS]:
                w.wait_ev(eng, key)

        if debug:
            w.op("q", lambda q: q.dma_start(out=dbg_ctx, in_=ctx_sb[:, :, :]), inc="SO", by=16)
            w.mark("dbg_ctx", "SO")
            w.op("q", lambda q: q.dma_start(out=dbg_x2, in_=x2_sb[:]), inc="QD", by=16)
            w.mark("dbg_x2", "QD")

        wd0, off0 = EXP_W[0], EXP_OFF[0]
        w.use("g", "w1_sb")
        w.op("g", lambda g: g.dma_start(
            out=w1_sb[:, :, 0:wd0],
            in_=w1_in.rearrange("(c p) wdim -> p c wdim", p=128)[:, :, off0:off0 + wd0]),
            inc="W1", by=16)
        w.mark("w1_ld:0", "W1")
        w.use("g", "w2_sb")
        w.op("g", lambda g: g.dma_start(
            out=w2_sb[:, 0:wd0 // 128, :],
            in_=w2_in.rearrange("(wc p) d -> p wc d", p=128)[:, 0:wd0 // 128, :]),
            inc="W2", by=16)
        w.mark("w2_ld:0", "W2")

        for tt in range(16):
            sl = tt % 3
            src = x2_sb[:, tt * KB:(tt + 1) * KB]
            w.wait_ev("v", f"apply2:{tt - 3}")
            w.op("v", lambda v, sl=sl, src=src: v.bn_stats(bn6[:, sl, 0:6], src[:, 0:512]))
            w.op("v", lambda v, sl=sl, src=src: v.bn_stats(bn6[:, sl, 6:12], src[:, 512:1024]))
            w.vdrain()
            w.op("v", lambda v, sl=sl: v.bn_aggr(mv_sb[:, sl, :], bn6[:, sl, :]), inc="V")
            w.mark(f"mv2:{tt}", "V")
            w.wait_ev("s", f"mv2:{tt}")
            w.op("s", lambda s, sl=sl: s.activation(rstd4[:, sl:sl + 1], mv_sb[:, sl, 1:2],
                                                    AF.Ln, bias=eps_t[:, 0:1]))
            w.sdrain()
            w.op("s", lambda s, sl=sl: s.activation(rstd4[:, sl:sl + 1], rstd4[:, sl:sl + 1],
                                                    AF.Exp, scale=-0.5), inc="Sc")
            w.mark(f"rstd2:{tt}", "Sc")
            w.wait_ev("v", f"rstd2:{tt}")
            w.vdrain()
            w.op("v", lambda v, sl=sl: v.scalar_tensor_tensor(
                nmr4[:, sl:sl + 1], mv_sb[:, sl, 0:1], -1.0, rstd4[:, sl:sl + 1],
                ALU.mult, ALU.mult), inc="V")
            w.mark(f"nmr2:{tt}", "V")
            w.wait_ev("s", f"nmr2:{tt}")
            w.use("s", f"xn2tmp{sl}")
            w.op("s", lambda s, sl=sl, src=src: s.activation(xn2tmp[sl][:], src, AF.Identity,
                                                    bias=nmr4[:, sl:sl + 1],
                                                    scale=rstd4[:, sl:sl + 1]), inc="Sc")
            w.mark(f"apply2:{tt}", "Sc")
            w.wait_ev("t", f"apply2:{tt}")
            w.wait_ev("t", f"stage_f32:{tt - 3}")
            for c in range(8):
                b = 2 * sl + c // 4
                if c % 4 == 0:
                    w.use("t", f"bank{b}")
                off = (c % 4) * 128
                w.op("t", lambda t, c=c, b=b, off=off, sl=sl: t.matmul(
                    pall[:, b, off:off + 128], xn2tmp[sl][:, c * 128:(c + 1) * 128],
                    ident_f32[:], start=(c % 4 == 0), stop=(c % 4 == 3),
                    is_transpose=True), inc="T" if c == 7 else None)
            w.mark(f"tr2:{tt}", "T")
            w.release(f"xn2tmp{sl}", f"tr2:{tt}")
            w.wait_ev("v", f"tr2:{tt}")
            w.use("v", f"stage{sl}")
            psrc = pall[:, 2 * sl:2 * sl + 2, :].rearrange("p a b -> p (a b)").rearrange(
                "p (c q) -> p c q", c=8)
            w.op("v", lambda v, psrc=psrc, sl=sl: v.tensor_copy(stage[sl][:, :, :], psrc),
                 inc="V")
            w.mark(f"stage_f32:{tt}", "V")
            w.op("v", lambda v, psrc=psrc, tt=tt: v.tensor_copy(
                xn2_fm[:, :, tt * 128:(tt + 1) * 128], psrc), inc="V")
            w.mark(f"xn2cast:{tt}", "V")
            w.release(f"bank{2 * sl}", f"xn2cast:{tt}")
            w.release(f"bank{2 * sl + 1}", f"xn2cast:{tt}")
            w.wait_ev("t", f"stage_f32:{tt}")
            w.wait_ev("t", "router_ld")
            if tt == 0:
                w.use("t", "bank6")
            for c in range(8):
                w.op("t", lambda t, c=c, sl=sl, tt=tt: t.matmul(
                    pall[:, 6, tt * 8:(tt + 1) * 8], stage[sl][:, c, :], router_sb[:, c, :],
                    start=(c == 0 and tt == 0), stop=(c == 7 and tt == 15)),
                    inc="T" if c == 7 else None)
            w.mark(f"router:{tt}", "T")
            w.release(f"stage{sl}", f"router:{tt}")
        w.wait_ev("s", "router:15")
        if w.t == "s":
            from concourse.hw_specs import get_activation_tables
            sidx = list(get_activation_tables(nc.m.arch)).index("sigmoid_and_others")
            w.h["s"].add_instruction(mybir.InstLoadActFuncSet(
                name=nc.get_next_instruction_name(), act_func_set_id=sidx, ins=[], outs=[]))
        w.op("s", lambda s: s.activation(probs_sb[:], pall[:, 6, 0:128], AF.Sigmoid), inc="Sc")
        w.mark("sigmoid", "Sc")
        w.release("bank6", "sigmoid")
        for tt in range(16):
            pr = probs_sb[:, tt * 8:(tt + 1) * 8]
            w.wait_ev("v", "sigmoid")
            w.op("v", lambda v, pr=pr: v.reduce_max(m1c[:], pr, axis=AX.X), inc="V")
            w.selfwait("v", "V")
            w.op("v", lambda v, pr=pr: v.tensor_single_scalar(msk1_t[:], pr, m1c[:, 0:1],
                                                              ALU.is_equal))
            w.vdrain()
            w.op("v", lambda v, pr=pr: v.scalar_tensor_tensor(p2_t[:], msk1_t[:], -2.0, pr,
                                                              ALU.mult, ALU.add), inc="V")
            w.selfwait("v", "V")
            w.op("v", lambda v: v.reduce_max(m2c[:], p2_t[:], axis=AX.X), inc="V")
            w.selfwait("v", "V")
            w.op("v", lambda v: v.tensor_single_scalar(p2_t[:], p2_t[:], m2c[:, 0:1],
                                                       ALU.is_equal))
            w.vdrain()
            w.op("v", lambda v: v.tensor_add(msk1_t[:], msk1_t[:], p2_t[:]))
            w.vdrain()
            w.vdrain()
            w.op("v", lambda v: v.scalar_tensor_tensor(swc[:], m1c[:], 1e-20, m2c[:],
                                                       ALU.add, ALU.add), inc="V")
            w.selfwait("v", "V")
            w.op("v", lambda v: v.reciprocal(rcc[:], swc[:]), inc="V")
            w.selfwait("v", "V")
            w.vdrain()
            w.op("v", lambda v, pr=pr, tt=tt: v.scalar_tensor_tensor(
                c_sb[:, tt * 8:(tt + 1) * 8], msk1_t[:], rcc[:, 0:1], pr, ALU.mult, ALU.mult),
                inc="V")
            w.mark(f"c:{tt}", "V")
        for sem_name, key in [("T", "ph4_T"), ("V", "ph4_V"), ("Sc", "ph4_S")]:
            w.mark(key, sem_name)
        for eng in ["t", "v", "s", "q", "g"]:
            for key in ["ph4_T", "ph4_V", "ph4_S"]:
                w.wait_ev(eng, key)

        if debug:
            w.op("q", lambda q: q.dma_start(out=dbg_s, in_=cap_s[:]), inc="QD", by=16)
            w.op("q", lambda q: q.dma_start(out=dbg_p, in_=cap_p[:, :, :]), inc="QD", by=16)
            w.op("q", lambda q: q.dma_start(out=dbg_xn2, in_=xn2_fm[:, :, :]), inc="QD", by=16)
            w.op("q", lambda q: q.dma_start(out=dbg_probs, in_=probs_sb[:]), inc="QD", by=16)
            w.op("q", lambda q: q.dma_start(out=dbg_c, in_=c_sb[:]), inc="QD", by=16)
            for i in range(2):
                w.op("q", lambda q, i=i: q.dma_start(out=dbg_tmp[i], in_=xn2tmp[i][:, :]),
                     inc="QD", by=16)
                w.op("q", lambda q, i=i: q.dma_start(out=dbg_stage[i], in_=stage[i][:, :, :]),
                     inc="QD", by=16)
            w.mark("dbg_ph4", "QD")
            w.wait_ev("v", "dbg_x2")
            w.wait_ev("v", "dbg_ph4")
            w.wait_ev("g", "dbg_ph4")
            w.wait_ev("g", "dbg_ctx")

        for e in range(8):
            wd = EXP_W[e]
            off = EXP_OFF[e]
            nwc = wd // 128
            if e == 0:
                continue_loads = False
            else:
                continue_loads = True
            if continue_loads:
                w.use("g", "w1_sb")
                w.op("g", lambda g, off=off, wd=wd: g.dma_start(
                    out=w1_sb[:, :, 0:wd],
                    in_=w1_in.rearrange("(c p) wdim -> p c wdim", p=128)[:, :, off:off + wd]),
                    inc="W1", by=16)
                w.mark(f"w1_ld:{e}", "W1")
                w.use("g", "w2_sb")
                w.op("g", lambda g, off=off, nwc=nwc: g.dma_start(
                    out=w2_sb[:, 0:nwc, :],
                    in_=w2_in.rearrange("(wc p) d -> p wc d", p=128)[:, off // 128: off // 128 + nwc, :]),
                    inc="W2", by=16)
                w.mark(f"w2_ld:{e}", "W2")
            for tb in range(4):
                for wc in range(nwc):
                    hb = wc % 2
                    w.wait_ev("t", f"w1_ld:{e}")
                    w.use("t", f"bank{hb}")
                    for c in range(8):
                        w.op("t", lambda t, c=c, wc=wc, tb=tb, hb=hb: t.matmul(
                            bank(hb), w1_sb[:, c, wc * 128:(wc + 1) * 128],
                            xn2_fm[:, c, tb * 512:(tb + 1) * 512],
                            start=(c == 0), stop=(c == 7)), inc="T" if c == 7 else None)
                    w.mark(f"hmm:{e}:{tb}:{wc}", "T")
                    if tb == 3 and wc == nwc - 1:
                        w.release("w1_sb", f"hmm:{e}:{tb}:{wc}")
                    w.wait_ev("s", f"hmm:{e}:{tb}:{wc}")
                    w.use("s", f"h1{hb}")
                    w.op("s", lambda s, hb=hb: s.activation(h1_s[hb][:], bank(hb), AF.Relu),
                         inc="Sc")
                    w.mark(f"relu:{e}:{tb}:{wc}", "Sc")
                    w.release(f"bank{hb}", f"relu:{e}:{tb}:{wc}")
                    w.wait_ev("v", f"relu:{e}:{tb}:{wc}")
                    if wc == 0:
                        w.use("v", "h2_sb")
                    w.op("v", lambda v, wc=wc, hb=hb: v.tensor_mul(
                        h2_sb[:, wc, :], h1_s[hb][:], h1_s[hb][:]), inc="V")
                    w.mark(f"h2sq:{e}:{tb}:{wc}", "V")
                    w.release(f"h1{hb}", f"h2sq:{e}:{tb}:{wc}")
                for tt in range(4):
                    ttg = tb * 4 + tt
                    for dh in range(2):
                        yb = 2 + (tt * 2 + dh) % 2
                        w.wait_ev("t", f"h2sq:{e}:{tb}:{nwc - 1}")
                        w.wait_ev("t", f"w2_ld:{e}")
                        w.use("t", f"bank{yb}")
                        for wc in range(nwc):
                            w.op("t", lambda t, wc=wc, tt=tt, dh=dh, yb=yb, nwc=nwc: t.matmul(
                                bank(yb), h2_sb[:, wc, tt * 128:(tt + 1) * 128],
                                w2_sb[:, wc, dh * 512:(dh + 1) * 512],
                                start=(wc == 0), stop=(wc == nwc - 1)),
                                inc="T" if wc == nwc - 1 else None)
                        w.mark(f"ymm:{e}:{tb}:{tt * 2 + dh}", "T")
                        if tb == 3 and tt == 3 and dh == 1:
                            w.release("w2_sb", f"ymm:{e}:{tb}:{tt * 2 + dh}")
                        w.wait_ev("v", f"ymm:{e}:{tb}:{tt * 2 + dh}")
                        w.op("v", lambda v, e=e, ttg=ttg, dh=dh, yb=yb: v.scalar_tensor_tensor(
                            x2_sb[:, ttg * KB + dh * 512: ttg * KB + (dh + 1) * 512],
                            bank(yb), c_sb[:, ttg * 8 + e: ttg * 8 + e + 1],
                            x2_sb[:, ttg * KB + dh * 512: ttg * KB + (dh + 1) * 512],
                            ALU.mult, ALU.add), inc="V")
                        w.mark(f"ysc:{e}:{ttg}:{dh}", "V")
                        w.release(f"bank{yb}", f"ysc:{e}:{ttg}:{dh}")
                w.release("h2_sb", f"ymm:{e}:{tb}:7")

        for ttg in range(16):
            w.wait_ev("q", f"ysc:7:{ttg}:1")
            w.op("q", lambda q, ttg=ttg: q.dma_start(
                out=out_d[ttg * 128:(ttg + 1) * 128, :],
                in_=x2_sb[:, ttg * KB:(ttg + 1) * KB]), inc="SO", by=16)
        w.wait("q", "SO", 16 * 16)

    @blk.sync
    def _(q):
        walk(Walk("q", {"q": q}, sems))

    @blk.gpsimd
    def _(g):
        walk(Walk("g", {"g": g}, sems))

    @blk.vector
    def _(v):
        walk(Walk("v", {"v": v}, sems))

    @blk.scalar
    def _(s):
        walk(Walk("s", {"s": s}, sems))

    @blk.tensor
    def _(t):
        walk(Walk("t", {"t": t}, sems))

    es.close()
    return nc


def _stage_inputs(L0_attn_x, L0_attn_w_qkv, L0_attn_w_out, L0_ffn_router_w,
                  L0_ffn_w1, L0_ffn_w2):
    x = np.ascontiguousarray(np.asarray(L0_attn_x, dtype=np.float32))
    w_qkv = np.asarray(L0_attn_w_qkv, dtype=np.float32)
    w_out = np.asarray(L0_attn_w_out, dtype=np.float32)
    router_w = np.asarray(L0_ffn_router_w, dtype=np.float32)
    w1 = np.ascontiguousarray(np.asarray(L0_ffn_w1, dtype=np.float32))
    w2 = np.ascontiguousarray(np.asarray(L0_ffn_w2, dtype=np.float32))

    wq = w_qkv[:, :, 0:64].reshape(D, 1024)
    wk = w_qkv[:, :, 64:128].reshape(D, 1024)
    wv = w_qkv[:, :, 128:192].reshape(D, 1024)
    wqkv = np.concatenate([wq, wk, wv], axis=1)
    wqkv_st = np.ascontiguousarray(
        wqkv.reshape(8, 128, 3072).transpose(1, 0, 2).reshape(128, 8 * 3072))
    wout_st = np.ascontiguousarray(
        w_out.reshape(8, 2, 64, KB).transpose(1, 2, 0, 3).reshape(128, 8 * KB))
    router_st = np.ascontiguousarray(
        router_w.T.reshape(8, 128, 8).transpose(1, 0, 2).reshape(128, 64))
    ident_bf = np.eye(128, dtype=ml_dtypes.bfloat16)
    ident_f32 = np.eye(128, dtype=np.float32)

    masks = {}
    ki = np.arange(128)[:, None]
    qj = np.arange(512)[None, :]
    for r in (0, 1):
        m = np.zeros((128, 4, 8, 512), dtype=ml_dtypes.bfloat16)
        for i in range(4):
            q0 = QBG[r][i] * 512
            for sl in range(8):
                ktg = TKS[i] - 8 + sl
                k0 = ktg * 128
                m[:, i, sl, :] = ((k0 + ki) <= (q0 + qj)).astype(ml_dtypes.bfloat16)
        masks[r] = m

    in_maps, row_maps = [], []
    for c in range(8):
        b, r = c // 2, c % 2
        rows = np.concatenate([g * 512 + np.arange(512) for g in QBG[r]])
        row_maps.append((b, rows))
        in_maps.append({
            "x_keys": x[b],
            "x_own": np.ascontiguousarray(x[b][rows]),
            "wqkv_in": wqkv_st,
            "wout_in": wout_st,
            "router_in": router_st,
            "w1_in": w1,
            "w2_in": w2,
            "masks_in": masks[r],
            "ident_bf_in": ident_bf,
            "ident_f32_in": ident_f32,
        })
    return in_maps, row_maps, x.shape


def kernel(L0_attn_x, L0_attn_w_qkv, L0_attn_w_out, L0_ffn_router_w,
           L0_ffn_w1, L0_ffn_w2):
    in_maps, row_maps, xshape = _stage_inputs(
        L0_attn_x, L0_attn_w_qkv, L0_attn_w_out, L0_ffn_router_w, L0_ffn_w1, L0_ffn_w2)
    nc = build_nc()
    res = run_bass_kernel_spmd(nc, in_maps, core_ids=list(range(8)))
    out = np.empty(xshape, dtype=np.float32)
    for c in range(8):
        b, rows = row_maps[c]
        out[b][rows] = np.asarray(res.results[c]["out"], dtype=np.float32)
    return out
